# version 04
# baseline (speedup 1.0000x reference)
import numpy as np
import concourse.bass as bass
import concourse.mybir as mybir
from concourse.bass_utils import run_bass_kernel_spmd

F32 = mybir.dt.float32
BF16 = mybir.dt.bfloat16
AF = mybir.ActivationFunctionType
ALU = mybir.AluOpType
AX = mybir.AxisListType

D = 1024
S = 4096
H = 16
HD = 64
DFF = 4096
WIN = 1024
NCD = D // 128
NCF = DFF // 128
NQC = S // 128
TG = 512
NTG = S // TG
FT = 1024
NFT = S // FT

LN_EPS = 1e-5
NEG_BIG = -1.0e9



def _blk_wT_full(w):
    wT = np.ascontiguousarray(np.transpose(w))
    nin, nout = wT.shape
    return np.ascontiguousarray(wT.reshape(nin // 128, 128, nout).transpose(1, 0, 2))


def _blk_w_oc(w, ocs=128):
    wT = np.ascontiguousarray(np.transpose(w))
    nin, nout = wT.shape
    a = wT.reshape(nin // 128, 128, nout // ocs, ocs).transpose(2, 1, 0, 3)
    return np.ascontiguousarray(a.reshape(nout // ocs, 128, (nin // 128) * ocs))


def _blk_vec(v):
    n = v.shape[0]
    return np.ascontiguousarray(v.reshape(n // 128, 128).T)


def _prep_weights(inp):
    w = {}
    for l in range(2):
        if l == 0:
            aw = {k: inp[f"L0_attn_{k}"] for k in
                  ["wq", "bq", "wk", "bk", "wv", "bv", "wo", "bo"]}
        else:
            aw = {"wq": inp["L1_attn_Wq"], "bq": inp["L1_attn_bq"],
                  "wk": inp["L1_attn_Wk"], "bk": inp["L1_attn_bk"],
                  "wv": inp["L1_attn_Wv"], "bv": inp["L1_attn_bv"],
                  "wo": inp["L1_attn_Wo"], "bo": inp["L1_attn_bo"]}
        f = {k: inp[f"L{l}_ffn_{k}"] for k in
             ["w1_w", "w1_s", "w2_w", "w2_s", "h1_w", "h1_s", "h2_w", "h2_s",
              "w3_w", "w3_s"]}
        w[f"wq{l}"] = _blk_wT_full(aw["wq"])
        w[f"wk{l}"] = _blk_wT_full(aw["wk"])
        w[f"wv{l}"] = _blk_wT_full(aw["wv"])
        w[f"wo{l}"] = _blk_wT_full(aw["wo"])
        w[f"bq{l}"] = _blk_vec(aw["bq"])
        w[f"bk{l}"] = _blk_vec(aw["bk"])
        w[f"bv{l}"] = _blk_vec(aw["bv"])
        w[f"bo{l}"] = _blk_vec(aw["bo"])
        w[f"w1{l}"] = _blk_wT_full(f["w1_w"])
        w[f"w2{l}"] = _blk_w_oc(f["w2_w"])
        w[f"h1{l}"] = _blk_w_oc(f["h1_w"])
        w[f"h2{l}"] = _blk_w_oc(f["h2_w"])
        w[f"w3{l}"] = _blk_w_oc(f["w3_w"])
        w[f"s1{l}"] = _blk_vec(f["w1_s"])
        w[f"s2{l}"] = _blk_vec(f["w2_s"])
        w[f"hs1{l}"] = _blk_vec(f["h1_s"])
        w[f"hs2{l}"] = _blk_vec(f["h2_s"])
        w[f"s3{l}"] = _blk_vec(f["w3_s"])
    return {k: np.ascontiguousarray(v, dtype=np.float32) for k, v in w.items()}



class P:

    def __init__(self, nc, dbg=None):
        self.nc = nc
        self.dbg = dbg or set()
        self.sem_i = 0
        self.phase_sems = []

    def sem(self, name):
        self.sem_i += 1
        h = self.nc.alloc_semaphore(f"{name}_{self.sem_i}")
        self.phase_sems.append(h)
        return h

    def recycle(self):
        if self.phase_sems:
            self.nc.clear_and_free_semaphores(self.phase_sems)
            self.nc.all_engine_barrier()
            self.phase_sems = []

    def uid(self):
        self.sem_i += 1
        return self.sem_i


def declare_io(p):
    nc = p.nc
    p.x = nc.dram_tensor("x", [NCD, 128, S], F32, kind="ExternalInput")
    p.y = nc.dram_tensor("y", [NCD, 128, S], F32, kind="ExternalOutput")
    p.wt = {}
    for l in range(2):
        for k in ["wq", "wk", "wv", "wo"]:
            p.wt[f"{k}{l}"] = nc.dram_tensor(f"{k}{l}", [128, NCD, D], F32,
                                             kind="ExternalInput")
        for k in ["bq", "bk", "bv", "bo"]:
            p.wt[f"{k}{l}"] = nc.dram_tensor(f"{k}{l}", [128, NCD], F32,
                                             kind="ExternalInput")
        p.wt[f"w1{l}"] = nc.dram_tensor(f"w1{l}", [128, NCD, DFF], F32,
                                        kind="ExternalInput")
        p.wt[f"w2{l}"] = nc.dram_tensor(f"w2{l}", [NCF, 128, NCD * 128], F32,
                                        kind="ExternalInput")
        p.wt[f"h1{l}"] = nc.dram_tensor(f"h1{l}", [NCF, 128, DFF], F32,
                                        kind="ExternalInput")
        p.wt[f"h2{l}"] = nc.dram_tensor(f"h2{l}", [NCF, 128, DFF], F32,
                                        kind="ExternalInput")
        p.wt[f"w3{l}"] = nc.dram_tensor(f"w3{l}", [NCD, 128, DFF], F32,
                                        kind="ExternalInput")
        for k in ["s1", "s2", "hs1", "hs2"]:
            p.wt[f"{k}{l}"] = nc.dram_tensor(f"{k}{l}", [128, NCF], F32,
                                             kind="ExternalInput")
        p.wt[f"s3{l}"] = nc.dram_tensor(f"s3{l}", [128, NCD], F32,
                                        kind="ExternalInput")
    p.x1 = nc.dram_tensor("x1", [NCD, 128, S], F32, kind="Internal")
    p.qf = nc.dram_tensor("qf", [NCD, 128, S], BF16, kind="Internal")
    p.kf = nc.dram_tensor("kf", [NCD, 128, S], BF16, kind="Internal")
    p.vf = nc.dram_tensor("vf", [NCD, 128, S], BF16, kind="Internal")
    p.of = nc.dram_tensor("of", [NCD, 128, S], BF16, kind="Internal")
    p.g = nc.dram_tensor("g", [NCF, 128, S], BF16, kind="Internal")
    p.h2a = nc.dram_tensor("h2a", [NCF, 128, S], BF16, kind="Internal")


def emit_const(p):
    nc = p.nc
    p.ones_f = nc.alloc_sbuf_tensor("ones_f", [128, 1], F32)
    p.ones_b = nc.alloc_sbuf_tensor("ones_b", [128, 1], BF16)
    p.onesr_f = nc.alloc_sbuf_tensor("onesr_f", [1, 128], F32)
    p.ident_b = nc.alloc_sbuf_tensor("ident_b", [128, 128], BF16)
    p.b0 = nc.alloc_sbuf_tensor("b0", [128, 1152], F32)
    p.ones4 = nc.alloc_sbuf_tensor("ones4", [128, 128], F32)
    p.mask16 = nc.alloc_sbuf_tensor("mask16", [128, H * 128], BF16)
    p.vec = {}
    for l in range(2):
        for k, w in [("bq", NCD), ("bk", NCD), ("bv", NCD), ("bo", NCD),
                     ("s1", NCF), ("s2", NCF), ("hs1", NCF), ("hs2", NCF),
                     ("s3", NCD)]:
            p.vec[f"{k}{l}"] = nc.alloc_sbuf_tensor(f"v_{k}{l}", [128, w], F32)
    p.bk0s = nc.alloc_sbuf_tensor("bk0s", [128, NCD], F32)

    ld = p.sem("c_ld")
    cs = p.sem("c_cs")
    n_ld = 0
    with nc.Block("const") as b:
        @b.sync
        def _(e):
            nonlocal n_ld
            for l in range(2):
                for k in ["bq", "bk", "bv", "bo", "s1", "s2", "hs1", "hs2", "s3"]:
                    e.dma_start(p.vec[f"{k}{l}"][:], p.wt[f"{k}{l}"].ap()
                                ).then_inc(ld, 16)
                    n_ld += 16

        @b.gpsimd
        def _(e):
            n = 0

            def step(inst):
                nonlocal n
                n += 1
                inst.then_inc(cs, 1)
                e.wait_ge(cs, n)

            step(e.memset(p.ones_f[:], 1.0))
            step(e.memset(p.ones_b[:], 1.0))
            step(e.memset(p.onesr_f[:], 1.0))
            step(e.memset(p.ones4[:], 1.0))
            step(e.memset(p.ident_b[:], 1.0))
            step(e.affine_select(out=p.ident_b[:], in_=p.ident_b[:],
                                 compare_op=ALU.is_ge, fill=0.0,
                                 base=0, pattern=[[-1, 128]], channel_multiplier=1))
            step(e.affine_select(out=p.ident_b[:], in_=p.ident_b[:],
                                 compare_op=ALU.is_ge, fill=0.0,
                                 base=0, pattern=[[1, 128]], channel_multiplier=-1))
            step(e.iota(p.b0[:], [[1, 1152]], base=-1024, channel_multiplier=-1,
                        allow_small_or_imprecise_dtypes=True))
            step(e.affine_select(out=p.b0[:], in_=p.b0[:], compare_op=ALU.is_ge,
                                 fill=NEG_BIG, base=1024, pattern=[[-1, 1152]],
                                 channel_multiplier=1))
            step(e.affine_select(out=p.b0[:], in_=p.b0[:], compare_op=ALU.is_ge,
                                 fill=NEG_BIG, base=-1, pattern=[[1, 1152]],
                                 channel_multiplier=-1))
            for hh in range(H):
                sl = p.mask16[:, hh * 128:(hh + 1) * 128]
                step(e.memset(sl, 1.0))
                step(e.affine_select(out=sl, in_=sl, compare_op=ALU.is_ge,
                                     fill=0.0, base=0, pattern=[[1, 128]],
                                     channel_multiplier=-1))

        @b.vector
        def _(e):
            e.wait_ge(ld, n_ld)
            e.tensor_scalar_mul(p.bk0s[:], p.vec["bk0"][:], 0.125)


def emit_ln(p, l, xn_sb):
    nc = p.nc
    src = p.x if l == 0 else p.x1
    ld = p.sem("ln_ld")
    sq = p.sem("ln_sq")
    s1d = p.sem("ln_s1")
    s2d = p.sem("ln_s2")
    mvar = p.sem("ln_mvar")
    rd = p.sem("ln_rd")
    dch = p.sem("ln_dch")
    bcd = p.sem("ln_bcd")
    xnd = p.sem("ln_xnd")

    u = p.uid()
    with (
        nc.sbuf_tensor(f"ln_xsb{u}", [128, NCD * TG], F32) as xsb,
        nc.sbuf_tensor(f"ln_sq{u}", [128, NCD * TG], F32) as sq_sb,
        nc.sbuf_tensor(f"ln_m{u}", [1, TG], F32) as m_sb,
        nc.sbuf_tensor(f"ln_e2{u}", [1, TG], F32) as e2_sb,
        nc.sbuf_tensor(f"ln_msq{u}", [1, TG], F32) as msq_sb,
        nc.sbuf_tensor(f"ln_var{u}", [1, TG], F32) as var_sb,
        nc.sbuf_tensor(f"ln_lnv{u}", [1, TG], F32) as lnv_sb,
        nc.sbuf_tensor(f"ln_r{u}", [1, TG], F32) as r_sb,
        nc.psum_tensor(f"ln_ps1{u}", [1, TG], F32) as ps1,
        nc.psum_tensor(f"ln_ps2{u}", [1, TG], F32) as ps2,
        nc.psum_tensor(f"ln_bcm{u}", [128, TG], F32) as bcm,
        nc.psum_tensor(f"ln_bcr{u}", [128, TG], F32) as bcr,
        nc.Block(f"ln{u}") as b,
    ):
        @b.sync
        def _(e):
            for t in range(NTG):
                if t > 0:
                    e.wait_ge(xnd, t)
                e.dma_start(
                    xsb[:].rearrange("p (c t) -> p c t", c=NCD),
                    src[:, :, t * TG:(t + 1) * TG].rearrange("c p t -> p c t"),
                ).then_inc(ld, 16)

        @b.scalar
        def _(e):
            for t in range(NTG):
                e.wait_ge(ld, 16 * (t + 1))
                e.activation(sq_sb[:], xsb[:], AF.Square).then_inc(sq, 1)
                e.wait_ge(mvar, t + 1)
                e.activation(lnv_sb[:], var_sb[:], AF.Ln).then_inc(rd, 1)
                e.wait_ge(rd, 2 * t + 1)
                e.activation(r_sb[:], lnv_sb[:], AF.Exp, scale=-0.5).then_inc(rd, 1)

        @b.tensor
        def _(e):
            for t in range(NTG):
                e.wait_ge(ld, 16 * (t + 1))
                if t > 0:
                    e.wait_ge(mvar, t)
                for c in range(NCD):
                    mm = e.matmul(ps1[:], p.ones_f[:], xsb[:, c * TG:(c + 1) * TG],
                                  start=(c == 0), stop=(c == NCD - 1))
                mm.then_inc(s1d, 1)
                e.wait_ge(sq, t + 1)
                for c in range(NCD):
                    mm = e.matmul(ps2[:], p.ones_f[:], sq_sb[:, c * TG:(c + 1) * TG],
                                  start=(c == 0), stop=(c == NCD - 1))
                mm.then_inc(s2d, 1)
                e.wait_ge(rd, 2 * (t + 1))
                if t > 0:
                    e.wait_ge(xnd, t)
                e.matmul(bcm[:], p.onesr_f[:], m_sb[:], start=True, stop=True)
                e.matmul(bcr[:], p.onesr_f[:], r_sb[:], start=True, stop=True
                         ).then_inc(bcd, 1)

        @b.vector
        def _(e):
            nch = 0
            for t in range(NTG):
                e.wait_ge(s1d, t + 1)
                e.tensor_scalar_mul(m_sb[:], ps1[:], 1.0 / D).then_inc(dch, 1)
                nch += 1
                e.wait_ge(s2d, t + 1)
                e.tensor_scalar(e2_sb[:], ps2[:], 1.0 / D, LN_EPS,
                                ALU.mult, ALU.add).then_inc(dch, 1)
                nch += 1
                e.wait_ge(dch, nch)
                e.tensor_mul(msq_sb[:], m_sb[:], m_sb[:]).then_inc(dch, 1)
                nch += 1
                e.wait_ge(dch, nch)
                e.tensor_sub(var_sb[:], e2_sb[:], msq_sb[:]).then_inc(mvar, 1)
                e.wait_ge(bcd, t + 1)
                for c in range(NCD):
                    cs = xsb[:, c * TG:(c + 1) * TG]
                    dst = xn_sb[:, c * S + t * TG: c * S + (t + 1) * TG]
                    e.tensor_sub(sq_sb[:, c * TG:(c + 1) * TG], cs, bcm[:]
                                 ).then_inc(dch, 1)
                    nch += 1
                    e.wait_ge(dch, nch)
                    tt = e.tensor_mul(dst, sq_sb[:, c * TG:(c + 1) * TG], bcr[:])
                tt.then_inc(xnd, 1)

    if f"xn{l}" in p.dbg:
        dump = nc.dram_tensor(f"dbg_xn{l}", [NCD, 128, S], BF16,
                              kind="ExternalOutput")
        sd = p.sem("ln_dump")
        with nc.Block(f"ln_dump{l}") as b:
            @b.sync
            def _(e):
                e.dma_start(dump.ap().rearrange("c p t -> p c t"),
                            xn_sb[:].rearrange("p (c t) -> p c t", c=NCD)
                            ).then_inc(sd, 16)
                e.wait_ge(sd, 16)


def emit_proj(p, l, wname, bname, dst, xn_sb, mode):
    nc = p.nc
    u = p.uid()
    wld = p.sem("pj_wld")
    mmd = p.sem("pj_mmd")
    evd = p.sem("pj_evd")
    std = p.sem("pj_std")
    bias = p.bk0s if mode == "scale8" else p.vec[f"{bname}{l}"]
    with (
        nc.sbuf_tensor(f"pj_w{u}", [128, NCD * D], BF16) as wsb,
        nc.sbuf_tensor(f"pj_st{u}", [128, 2 * S], BF16) as stg,
        nc.psum_tensor(f"pj_ps{u}", [128, 2 * TG], F32) as ps,
        nc.Block(f"pj{u}") as b,
    ):
        @b.gpsimd
        def _(e):
            e.dma_start(wsb[:], p.wt[f"{wname}{l}"].ap().rearrange("p c o -> p (c o)"),
                        max_dma_last_dim=8192).then_inc(wld, 16)

        @b.tensor
        def _(e):
            e.wait_ge(wld, 16)
            un = 0
            for oc in range(NCD):
                for tg in range(NTG):
                    un += 1
                    if un > 2:
                        e.wait_ge(evd, un - 2)
                    pv = ps[:, (un % 2) * TG:(un % 2) * TG + TG]
                    for ic in range(NCD):
                        mm = e.matmul(
                            pv,
                            wsb[:, ic * D + oc * 128: ic * D + oc * 128 + 128],
                            xn_sb[:, ic * S + tg * TG: ic * S + (tg + 1) * TG],
                            start=(ic == 0), stop=(ic == NCD - 1))
                    mm.then_inc(mmd, 1)

        @b.scalar
        def _(e):
            un = 0
            for oc in range(NCD):
                if oc >= 2:
                    e.wait_ge(std, 16 * (oc - 1))
                for tg in range(NTG):
                    un += 1
                    e.wait_ge(mmd, un)
                    pv = ps[:, (un % 2) * TG:(un % 2) * TG + TG]
                    out = stg[:, (oc % 2) * S + tg * TG:(oc % 2) * S + (tg + 1) * TG]
                    bsl = bias[:, oc:oc + 1]
                    if mode == "exp":
                        a = e.activation(out, pv, AF.Exp, bias=bsl)
                    elif mode == "scale8":
                        a = e.activation(out, pv, AF.Identity, bias=bsl, scale=0.125)
                    else:
                        a = e.activation(out, pv, AF.Identity, bias=bsl)
                    a.then_inc(evd, 1)

        @b.sync
        def _(e):
            for oc in range(NCD):
                e.wait_ge(evd, oc * NTG + NTG)
                e.dma_start(dst[oc, :, :], stg[:, (oc % 2) * S:(oc % 2) * S + S]).then_inc(std, 16)
                e.wait_ge(std, 16 * (oc + 1))


def emit_qkv(p, l, xn_sb):
    if l == 0:
        specs = [("wq", "bq", p.qf, "id"), ("wk", "bk", p.kf, "scale8"),
                 ("wv", "bv", p.vf, "id")]
    else:
        specs = [("wq", "bq", p.qf, "exp"), ("wk", "bk", p.kf, "exp"),
                 ("wv", "bv", p.vf, "id")]
    for wn, bn, dst, mode in specs:
        emit_proj(p, l, wn, bn, dst, xn_sb, mode)
        p.recycle()
    for nm, t in [("qf", p.qf), ("kf", p.kf), ("vf", p.vf)]:
        if f"{nm}{l}" in p.dbg:
            dump = p.nc.dram_tensor(f"dbg_{nm}{l}", [NCD, 128, S], BF16,
                                    kind="ExternalOutput")
            sd = p.sem("qkv_dump")
            with p.nc.Block() as b:
                @b.sync
                def _(e):
                    e.dma_start(dump.ap(), t.ap()).then_inc(sd, 16)
                    e.wait_ge(sd, 16)


def emit_attn0(p):
    import math
    nc = p.nc
    u = p.uid()
    slopes = []
    start = 2 ** (-2 ** (-(math.log2(H) - 3)))
    for i in range(H):
        slopes.append(float(start * start ** i))

    qld = p.sem("a0_qld")
    kld = p.sem("a0_kld")
    vld = p.sem("a0_vld")
    vtd = p.sem("a0_vtd")
    mmS = p.sem("a0_mmS")
    biasd = p.sem("a0_bias")
    maxd = p.sem("a0_maxd")
    expd = p.sem("a0_expd")
    recd = p.sem("a0_recd")
    normd = p.sem("a0_norm")
    pta = p.sem("a0_pta")
    ptev = p.sem("a0_ptev")
    pvd = p.sem("a0_pvd")
    oev = p.sem("a0_oev")
    ofd = p.sem("a0_ofd")

    RING = 10

    def nkc(qc):
        return min(qc + 1, 9)

    pta_cum = [0]
    ptev_cum_head = []
    ptev_cum_qc = [0]
    ptev_c = 0
    pta_c = 0
    for qc in range(NQC):
        rounds = 1 + (1 if nkc(qc) > 8 else 0)
        ptev_c += 1
        for h in range(H):
            pta_c += rounds
            pta_cum.append(pta_c)
            ptev_c += rounds
            ptev_cum_head.append(ptev_c)
        ptev_cum_qc.append(ptev_c)

    with (
        nc.sbuf_tensor(f"a0_q{u}", [128, 2 * D], BF16) as q_sb,
        nc.sbuf_tensor(f"a0_vs{u}", [128, 2 * D], BF16) as vs,
        nc.sbuf_tensor(f"a0_kr{u}", [128, NCD * RING * 128], BF16) as k_ring,
        nc.sbuf_tensor(f"a0_vtr{u}", [128, RING * H * HD], BF16) as vt_ring,
        nc.sbuf_tensor(f"a0_ssb{u}", [128, 2 * 1152], F32) as s_sb,
        nc.sbuf_tensor(f"a0_psb{u}", [128, 2 * 1152], BF16) as p_sb,
        nc.sbuf_tensor(f"a0_pnb{u}", [128, 2 * 1152], BF16) as pn_sb,
        nc.sbuf_tensor(f"a0_ptsb{u}", [128, 1152], BF16) as pt_sb,
        nc.sbuf_tensor(f"a0_negm{u}", [128, 2], F32) as negm,
        nc.sbuf_tensor(f"a0_sum{u}", [128, 2], F32) as sums,
        nc.sbuf_tensor(f"a0_rin{u}", [128, 2], F32) as rinv,
        nc.sbuf_tensor(f"a0_ofs{u}", [128, 2 * D], BF16) as ofs,
        nc.psum_tensor(f"a0_sa{u}", [128, 1152], F32) as s_psA,
        nc.psum_tensor(f"a0_sb{u}", [128, 1152], F32) as s_psB,
        nc.psum_tensor(f"a0_pt{u}", [128, 1024], BF16) as pt_ps,
        nc.psum_tensor(f"a0_o{u}", [128, 512], F32) as o_ps,
        nc.Block(f"a0_{u}") as b,
    ):
        s_ps = [s_psA, s_psB]

        def q_sl(qc, h):
            base = (h % 2) * 64
            return q_sb[base:base + 64,
                        (qc % 2) * D + (h // 2) * 128:(qc % 2) * D + (h // 2) * 128 + 128]

        def k_sl(h, slot):
            base = (h % 2) * 64
            c = h // 2
            return k_ring[base:base + 64,
                          c * (RING * 128) + slot * 128: c * (RING * 128) + slot * 128 + 128]

        def vt_sl(h, slot):
            base = slot * (NCD * 128) + (h // 2) * 128 + (h % 2) * 64
            return vt_ring[:, base: base + HD]

        @b.sync
        def _(e):
            for qc in range(NQC):
                if qc >= 2:
                    e.wait_ge(mmS, 16 * (qc - 1))
                e.dma_start(
                    q_sb[:, (qc % 2) * D:(qc % 2) * D + D].rearrange(
                        "p (c j) -> p c j", c=NCD),
                    p.qf[:, :, qc * 128:(qc + 1) * 128].rearrange("c p t -> p c t"),
                ).then_inc(qld, 16)
                e.wait_ge(qld, 16 * (qc + 1))
                slot = qc % RING
                if qc >= 2:
                    e.wait_ge(pvd, 16 * (qc - 1))
                e.dma_start(
                    k_ring[:].rearrange("p (c r) -> p c r", c=NCD)[
                        :, :, slot * 128:(slot + 1) * 128],
                    p.kf[:, :, qc * 128:(qc + 1) * 128].rearrange("c p t -> p c t"),
                ).then_inc(kld, 16)
                e.wait_ge(kld, 16 * (qc + 1))
                if qc >= 2:
                    e.wait_ge(vtd, qc - 1)
                e.dma_start(
                    vs[:, (qc % 2) * D:(qc % 2) * D + D].rearrange(
                        "p (c j) -> p c j", c=NCD),
                    p.vf[:, :, qc * 128:(qc + 1) * 128].rearrange("c p t -> p c t"),
                ).then_inc(vld, 16)
                e.wait_ge(vld, 16 * (qc + 1))
                e.wait_ge(oev, 2 * qc + 2)
                e.dma_start(
                    p.of[:, :, qc * 128:(qc + 1) * 128].rearrange("c p t -> p c t"),
                    ofs[:, (qc % 2) * D:(qc % 2) * D + D].rearrange(
                        "p (c j) -> p c j", c=NCD),
                ).then_inc(ofd, 16)
                e.wait_ge(ofd, 16 * (qc + 1))
            e.wait_ge(ofd, 16 * NQC)

        @b.tensor
        def _(e):
            for qc in range(NQC):
                n = nkc(qc)
                w = n * 128
                kstart = max(0, qc - 8)
                gh0 = qc * 16
                e.wait_ge(vld, 16 * (qc + 1))
                e.wait_ge(ptev, ptev_cum_qc[qc])
                for c in range(NCD):
                    tr = e.transpose(
                        pt_ps[:, c * 128:(c + 1) * 128],
                        vs[:, (qc % 2) * D + c * 128:(qc % 2) * D + c * 128 + 128],
                        p.ident_b[:])
                tr.then_inc(vtd, 1)

                e.wait_ge(qld, 16 * (qc + 1))
                e.wait_ge(kld, 16 * (qc + 1))

                def scores(h):
                    gh = gh0 + h
                    if gh >= 2:
                        e.wait_ge(biasd, gh - 1)
                    for j in range(n):
                        kc = kstart + j
                        mm = e.matmul(
                            s_ps[gh % 2][:, j * 128:(j + 1) * 128],
                            q_sl(qc, h), k_sl(h, kc % RING),
                            start=True, stop=True)
                    mm.then_inc(mmS, 1)

                scores(0)
                if H > 1:
                    scores(1)
                for h in range(H):
                    gh = gh0 + h
                    e.wait_ge(normd, gh + 1)
                    nra = min(n, 8)
                    e.wait_ge(ptev, (ptev_cum_head[gh - 1] if gh > gh0
                                     else ptev_cum_qc[qc] + 1))
                    for j in range(nra):
                        tr = e.transpose(
                            pt_ps[:, j * 128:(j + 1) * 128],
                            pn_sb[:, (h % 2) * 1152 + j * 128:(h % 2) * 1152 + (j + 1) * 128],
                            p.ident_b[:])
                    tr.then_inc(pta, 1)
                    if n > 8:
                        e.wait_ge(ptev, ptev_cum_head[gh] - 1)
                        tr = e.transpose(
                            pt_ps[:, 0:128],
                            pn_sb[:, (h % 2) * 1152 + 8 * 128:(h % 2) * 1152 + 9 * 128],
                            p.ident_b[:])
                        tr.then_inc(pta, 1)
                    e.wait_ge(ptev, ptev_cum_head[gh])
                    if h == 0 or h == 8:
                        e.wait_ge(oev, 2 * qc + (1 if h == 8 else 0))
                    obase = (h % 2) * 64
                    ocol = ((h // 2) % 4) * 128
                    for j in range(n):
                        kc = kstart + j
                        mm = e.matmul(
                            o_ps[obase:obase + 64, ocol:ocol + 128],
                            vt_sl(h, kc % RING),
                            pt_sb[:, j * 128:(j + 1) * 128],
                            start=(j == 0), stop=(j == n - 1))
                    mm.then_inc(pvd, 1)
                    if h + 2 < H:
                        scores(h + 2)

        @b.vector
        def _(e):
            for qc in range(NQC):
                n = nkc(qc)
                w = n * 128
                for h in range(H):
                    gh = qc * 16 + h
                    pe_parity = gh % 2
                    e.wait_ge(mmS, gh + 1)
                    if gh >= 2:
                        e.wait_ge(expd, gh - 1)
                    ssl = s_sb[:, pe_parity * 1152: pe_parity * 1152 + w]
                    e.scalar_tensor_tensor(
                        ssl, p.b0[:, 1152 - w:], slopes[h],
                        s_ps[pe_parity][:, 0:w],
                        ALU.mult, ALU.add).then_inc(biasd, 1)
                    e.wait_ge(biasd, gh + 1)
                    e.tensor_reduce(negm[:, pe_parity:pe_parity + 1], ssl,
                                    AX.X, ALU.max, negate=True).then_inc(maxd, 1)
                    e.wait_ge(expd, gh + 1)
                    e.reciprocal(rinv[:, pe_parity:pe_parity + 1],
                                 sums[:, pe_parity:pe_parity + 1]).then_inc(recd, 1)

        @b.scalar
        def _(e):
            def ptevac(qc, h):
                gh = qc * 16 + h
                n = nkc(qc)
                nra = min(n, 8)
                e.wait_ge(pta, pta_cum[gh] + 1)
                e.activation(pt_sb[:, 0:nra * 128], pt_ps[:, 0:nra * 128],
                             AF.Identity).then_inc(ptev, 1)
                if n > 8:
                    e.wait_ge(pta, pta_cum[gh + 1])
                    e.activation(pt_sb[:, 1024:1152], pt_ps[:, 0:128],
                                 AF.Identity).then_inc(ptev, 1)

            def oevac(qc, half):
                e.wait_ge(pvd, qc * 16 + 8 * (half + 1))
                if qc >= 2 and half == 0:
                    e.wait_ge(ofd, 16 * (qc - 1))
                e.activation(
                    ofs[:, (qc % 2) * D + half * 512:(qc % 2) * D + half * 512 + 512],
                    o_ps[:], AF.Identity).then_inc(oev, 1)

            for qc in range(NQC):
                n = nkc(qc)
                w = n * 128
                e.wait_ge(vtd, qc + 1)
                if qc >= 2:
                    e.wait_ge(pvd, 16 * (qc - 1))
                slot = qc % RING
                e.activation(
                    vt_ring[:, slot * (H * HD): (slot + 1) * (H * HD)],
                    pt_ps[:, 0:H * HD], AF.Identity).then_inc(ptev, 1)
                for h in range(H):
                    gh = qc * 16 + h
                    parity = gh % 2
                    e.wait_ge(maxd, gh + 1)
                    if gh >= 2:
                        e.wait_ge(normd, gh - 1)
                    e.activation(
                        p_sb[:, parity * 1152: parity * 1152 + w],
                        s_sb[:, parity * 1152: parity * 1152 + w],
                        AF.Exp, bias=negm[:, parity:parity + 1],
                        accum_out=sums[:, parity:parity + 1]).then_inc(expd, 1)
                    if h >= 1:
                        ptevac(qc, h - 1)
                        if h - 1 == 7:
                            oevac(qc, 0)
                ptevac(qc, H - 1)
                oevac(qc, 1)

        @b.gpsimd
        def _(e):
            for qc in range(NQC):
                n = nkc(qc)
                w = n * 128
                for h in range(H):
                    gh = qc * 16 + h
                    parity = gh % 2
                    e.wait_ge(recd, gh + 1)
                    if gh >= 2:
                        e.wait_ge(pta, pta_cum[gh - 1])
                    e.tensor_scalar_mul(
                        pn_sb[:, parity * 1152: parity * 1152 + w],
                        p_sb[:, parity * 1152: parity * 1152 + w],
                        rinv[:, parity:parity + 1]).then_inc(normd, 1)

    _dump(p, "of0", p.of, [NCD, 128, S], BF16)


def emit_attn1(p):
    nc = p.nc
    u = p.uid()
    qld = p.sem("a1_qld")
    kld = p.sem("a1_kld")
    vld = p.sem("a1_vld")
    ktd = p.sem("a1_ktd")
    vtd = p.sem("a1_vtd")
    tpev = p.sem("a1_tpev")
    atd = p.sem("a1_atd")
    maskd = p.sem("a1_maskd")
    od = p.sem("a1_od")
    dend = p.sem("a1_dend")
    ud = p.sem("a1_ud")
    rdend = p.sem("a1_rdend")
    bcd = p.sem("a1_bcd")
    oevd = p.sem("a1_oevd")
    divd = p.sem("a1_divd")
    kvd = p.sem("a1_kvd")
    dch = p.sem("a1_dch")
    inis = p.sem("a1_inis")
    ofd = p.sem("a1_ofd")

    NC = NQC
    repl2 = set()
    for d in p.dbg:
        if isinstance(d, str) and d.startswith("a1r:"):
            repl2.add(d.split(":")[1])

    with (
        nc.sbuf_tensor(f"a1_q{u}", [128, 2 * D], BF16) as q_cols,
        nc.sbuf_tensor(f"a1_k{u}", [128, 2 * D], BF16) as k_cols,
        nc.sbuf_tensor(f"a1_v{u}", [128, 2 * D], BF16) as v_cols,
        nc.sbuf_tensor(f"a1_kt{u}", [128, 2 * D], BF16) as kt_sb,
        nc.sbuf_tensor(f"a1_vt{u}", [128, 2 * D], BF16) as vt_sb,
        nc.sbuf_tensor(f"a1_at{u}", [128, H * 128], BF16) as at_sb,
        nc.sbuf_tensor(f"a1_osb{u}", [128, NCD * 128], F32) as o_sb,
        nc.sbuf_tensor(f"a1_rd{u}", [128, NCD * 128], F32) as rden_sb,
        nc.sbuf_tensor(f"a1_ofs{u}", [128, 2 * D], BF16) as of_sb,
        nc.sbuf_tensor(f"a1_kv32{u}", [128, 512], F32) as kv32,
        nc.sbuf_tensor(f"a1_kvbf{u}", [128, 512], BF16) as kvbf,
        nc.sbuf_tensor(f"a1_k1c{u}", [128, NCD], F32) as k1c,
        nc.sbuf_tensor(f"a1_k132{u}", [128, NCD], F32) as k132,
        nc.sbuf_tensor(f"a1_k1bf{u}", [128, NCD], BF16) as k1bf,
        nc.psum_tensor(f"a1_tp{u}", [128, 1024], BF16) as tp_ps,
        nc.psum_tensor(f"a1_atp{u}", [128, 1024], F32) as at_ps,
        nc.psum_tensor(f"a1_op{u}", [128, 1024], F32) as o_ps,
        nc.psum_tensor(f"a1_up{u}", [128, 512], F32) as u_ps,
        nc.psum_tensor(f"a1_dp{u}", [128, 1024], F32) as den_ps,
        nc.Block(f"a1_{u}") as b,
    ):
        def q_sl(h, par):
            base = (h % 2) * 64
            return q_cols[base:base + 64,
                          par * D + (h // 2) * 128: par * D + (h // 2) * 128 + 128]

        def k_sl(h, par):
            base = (h % 2) * 64
            return k_cols[base:base + 64,
                          par * D + (h // 2) * 128: par * D + (h // 2) * 128 + 128]

        def kt_sl(h, par):
            base = par * D + (h // 2) * 128 + (h % 2) * 64
            return kt_sb[:, base: base + HD]

        def vt_sl(h, par):
            base = par * D + (h // 2) * 128 + (h % 2) * 64
            return vt_sb[:, base: base + HD]

        def kv_sl(t, h):
            return t[(h % 2) * 64:(h % 2) * 64 + 64,
                     (h // 2) * HD:(h // 2) * HD + HD]

        def o_sl(t, h):
            return t[(h % 2) * 64:(h % 2) * 64 + 64,
                     (h // 2) * 128:(h // 2) * 128 + 128]

        def den_sl(t, h):
            r = (h % 2) * 64
            return t[r:r + 1, (h // 2) * 128:(h // 2) * 128 + 128]

        def at_col(hh):
            return (hh % 2) * 512 + (hh // 2) * 128

        @b.sync
        def _(e):
            for c in range(NC):
                par = c % 2
                if c >= 2:
                    e.wait_ge(dend, 2 * (c - 1))
                e.dma_start(
                    q_cols[:, par * D:par * D + D].rearrange("p (c j) -> p c j", c=NCD),
                    p.qf[:, :, c * 128:(c + 1) * 128].rearrange("c p t -> p c t"),
                ).then_inc(qld, 16)
                e.wait_ge(qld, 16 * (c + 1))
                if c >= 2:
                    e.wait_ge(ktd, c - 1)
                    e.wait_ge(dch, 5 * (c - 2) + 4)
                e.dma_start(
                    k_cols[:, par * D:par * D + D].rearrange("p (c j) -> p c j", c=NCD),
                    p.kf[:, :, c * 128:(c + 1) * 128].rearrange("c p t -> p c t"),
                ).then_inc(kld, 16)
                e.wait_ge(kld, 16 * (c + 1))
                if c >= 2:
                    e.wait_ge(vtd, c - 1)
                e.dma_start(
                    v_cols[:, par * D:par * D + D].rearrange("p (c j) -> p c j", c=NCD),
                    p.vf[:, :, c * 128:(c + 1) * 128].rearrange("c p t -> p c t"),
                ).then_inc(vld, 16)
                e.wait_ge(vld, 16 * (c + 1))
                e.wait_ge(divd, c + 1)
                if "noof" in repl2:
                    e.dma_start(p.of[0, :, 0:64], of_sb[:, 0:64]).then_inc(ofd, 16)
                else:
                    e.dma_start(
                        p.of[:, :, c * 128:(c + 1) * 128].rearrange("c p t -> p c t"),
                        of_sb[:, par * D:par * D + D].rearrange("p (c j) -> p c j", c=NCD),
                    ).then_inc(ofd, 16)
                e.wait_ge(ofd, 16 * (c + 1))

        cut = 99
        repl = set()
        for d in p.dbg:
            if isinstance(d, str) and d.startswith("a1cut"):
                cut = int(d.split(":")[1])
            if isinstance(d, str) and d.startswith("a1r:"):
                repl.add(d.split(":")[1])

        @b.tensor
        def _(e):
            def stub(region2, par, width=512):
                return e.matmul(region2, k_cols[:, par * D:par * D + 128],
                                q_cols[:, par * D:par * D + width],
                                start=True, stop=True)

            for c in range(NC):
                par = c % 2
                e.wait_ge(kld, 16 * (c + 1))
                e.wait_ge(tpev, 2 * c)
                if "notr" in repl:
                    e.matmul(u_ps[:], k_cols[:, par * D:par * D + 128],
                             q_cols[:, par * D:par * D + 512],
                             start=True, stop=True).then_inc(ktd, 1)
                else:
                    for cb in range(NCD):
                        tr = e.transpose(tp_ps[:, cb * 128:(cb + 1) * 128],
                                         k_cols[:, par * D + cb * 128:par * D + cb * 128 + 128],
                                         p.ident_b[:])
                    tr.then_inc(ktd, 1)
                e.wait_ge(vld, 16 * (c + 1))
                e.wait_ge(tpev, 2 * c + 1)
                if "notr" in repl:
                    e.matmul(u_ps[:], k_cols[:, par * D:par * D + 128],
                             q_cols[:, par * D:par * D + 512],
                             start=True, stop=True).then_inc(vtd, 1)
                else:
                    for cb in range(NCD):
                        tr = e.transpose(tp_ps[:, cb * 128:(cb + 1) * 128],
                                         v_cols[:, par * D + cb * 128:par * D + cb * 128 + 128],
                                         p.ident_b[:])
                    tr.then_inc(vtd, 1)
                e.wait_ge(qld, 16 * (c + 1))
                for half in range(2):
                    if c > 0 and half == 0:
                        e.wait_ge(divd, c)
                    if half == 1:
                        e.wait_ge(maskd, 2 * c + 1)
                    if cut >= 2:
                        for hh in range(8):
                            h = half * 8 + hh
                            mm = e.matmul(at_ps[:, at_col(hh):at_col(hh) + 128],
                                          k_sl(h, par), q_sl(h, par),
                                          start=True, stop=True)
                    else:
                        stub(at_ps[:, 0:512], par)
                        mm = stub(at_ps[:, 512:1024], par)
                    mm.then_inc(atd, 1)
                    e.wait_ge(maskd, 2 * c + half + 1)
                    e.wait_ge(tpev, 2 * c + 2)
                    e.wait_ge(kvd, c + 1)
                    if half == 0:
                        e.wait_ge(oevd, c)
                        e.wait_ge(rdend, c)
                    if cut >= 3:
                        for hh in range(8):
                            h = half * 8 + hh
                            acol = half * 1024 + at_col(hh)
                            e.matmul(o_sl(o_ps, h), vt_sl(h, par),
                                     at_sb[:, acol:acol + 128],
                                     start=True, stop=False)
                            e.matmul(o_sl(o_ps, h), kv_sl(kvbf, h), q_sl(h, par),
                                     start=False, stop=True)
                            if cut >= 4:
                                e.matmul(den_sl(den_ps, h), p.ones_b[:],
                                         at_sb[:, acol:acol + 128],
                                         start=True, stop=False)
                                mm = e.matmul(den_sl(den_ps, h),
                                              k1bf[(h % 2) * 64:(h % 2) * 64 + 64,
                                                   h // 2:h // 2 + 1],
                                              q_sl(h, par),
                                              start=False, stop=True)
                            else:
                                mm = stub(den_ps[:, half * 512:half * 512 + 512], par)
                    else:
                        stub(o_ps[:, half * 512:half * 512 + 512], par)
                        mm = stub(den_ps[:, half * 512:half * 512 + 512], par)
                    mm.then_inc(dend, 1)
                if cut >= 5:
                    for h in range(H):
                        mm = e.matmul(kv_sl(u_ps, h), kt_sl(h, par), vt_sl(h, par),
                                      start=True, stop=True)
                else:
                    mm = stub(u_ps[:], par)
                mm.then_inc(ud, 1)
                e.wait_ge(rdend, c + 1)
                e.wait_ge(maskd, 2 * c + 2)
                if cut >= 6:
                    for h in range(0, H, 2):
                        mm = e.matmul(o_sl(at_ps, h),
                                      p.ones4[0:1, 0:64],
                                      rden_sb[0:1, (h // 2) * 128:(h // 2) * 128 + 128],
                                      start=True, stop=True)
                    mm = e.matmul(den_ps[:, 0:64], k_cols[:, par * D:par * D + 128],
                                  q_cols[:, par * D:par * D + 64],
                                  start=True, stop=True)
                    for h in range(1, H, 2):
                        mm = e.matmul(o_sl(at_ps, h),
                                      p.ones4[64:65, 0:64],
                                      rden_sb[64:65, (h // 2) * 128:(h // 2) * 128 + 128],
                                      start=True, stop=True)
                else:
                    stub(at_ps[:, 0:512], par)
                    mm = stub(at_ps[:, 512:1024], par)
                mm.then_inc(bcd, 1)

        @b.scalar
        def _(e):
            for c in range(NC):
                par = c % 2
                e.wait_ge(ktd, c + 1)
                if c >= 2:
                    e.wait_ge(ud, c - 1)
                e.activation(kt_sb[:, par * D:par * D + D], tp_ps[:],
                             AF.Identity).then_inc(tpev, 1)
                e.wait_ge(vtd, c + 1)
                e.activation(vt_sb[:, par * D:par * D + D], tp_ps[:],
                             AF.Identity).then_inc(tpev, 1)
                e.wait_ge(dend, 2 * c + 2)
                e.wait_ge(divd, c)
                if "oev" in repl:
                    e.activation(o_sb[:, 0:512], o_ps[:, 0:512], AF.Identity
                                 ).then_inc(tpev, 0, skip_validation=True) if False else None
                    e.activation(o_sb[:, 0:512], o_ps[:, 0:512], AF.Identity)
                    e.activation(o_sb[:, 512:1024], o_ps[:, 512:1024],
                                 AF.Identity).then_inc(oevd, 1)
                else:
                    e.activation(o_sb[:], o_ps[:], AF.Identity).then_inc(oevd, 1)

        @b.vector
        def _(e):
            nd = 0
            for c in range(NC):
                par = c % 2
                for half in range(2):
                    e.wait_ge(atd, 2 * c + half + 1)
                    e.tensor_mul(at_sb[:, half * 1024:(half + 1) * 1024],
                                 at_ps[:], p.mask16[:, 0:1024]).then_inc(maskd, 1)
                e.wait_ge(dend, 2 * c + 2)
                if c > 0:
                    e.wait_ge(bcd, c)
                if "recip" in repl:
                    e.tensor_copy(rden_sb[0:1, :], den_ps[0:1, :]).then_inc(dch, 1)
                    nd += 1
                    e.tensor_copy(rden_sb[64:65, :], den_ps[64:65, :]).then_inc(rdend, 1)
                else:
                    e.reciprocal(rden_sb[0:1, :], den_ps[0:1, :]).then_inc(dch, 1)
                    nd += 1
                    e.reciprocal(rden_sb[64:65, :], den_ps[64:65, :]).then_inc(rdend, 1)
                e.wait_ge(bcd, c + 1)
                e.wait_ge(oevd, c + 1)
                if c >= 2:
                    e.wait_ge(ofd, 16 * (c - 1))
                e.tensor_mul(of_sb[:, par * D:par * D + D], o_sb[:],
                             at_ps[:]).then_inc(divd, 1)
                e.wait_ge(ud, c + 1)
                if c == 0:
                    e.wait_ge(kvd, 1)
                if "kvadd" in repl:
                    e.tensor_copy(kv32[:], u_ps[:]).then_inc(dch, 1)
                else:
                    e.tensor_add(kv32[:], kv32[:], u_ps[:]).then_inc(dch, 1)
                nd += 1
                e.wait_ge(dch, nd)
                e.tensor_copy(kvbf[:], kv32[:]).then_inc(dch, 1)
                nd += 1
                e.wait_ge(kld, 16 * (c + 1))
                if "red" in repl:
                    e.memset(k1c[:], 1.0).then_inc(dch, 1)
                else:
                    e.tensor_reduce(k1c[:], k_cols[:, par * D:par * D + D].rearrange(
                        "p (c j) -> p c j", c=NCD), AX.X, ALU.add).then_inc(dch, 1)
                nd += 1
                e.wait_ge(dch, nd)
                if "kvadd" in repl:
                    e.tensor_copy(k132[:], k1c[:]).then_inc(dch, 1)
                else:
                    e.tensor_add(k132[:], k132[:], k1c[:]).then_inc(dch, 1)
                nd += 1
                e.wait_ge(dch, nd)
                e.tensor_copy(k1bf[:], k132[:]).then_inc(kvd, 1)

        if "nogp" not in repl2:
            @b.gpsimd
            def _(e):
                e.memset(kv32[:], 0.0).then_inc(inis, 1)
                e.wait_ge(inis, 1)
                e.memset(kvbf[:], 0.0).then_inc(inis, 1)
                e.wait_ge(inis, 2)
                e.memset(k132[:], 0.0).then_inc(inis, 1)
                e.wait_ge(inis, 3)
                e.memset(k1bf[:], 0.0).then_inc(kvd, 1)
        else:
            @b.vector
            def _(e):
                e.memset(kv32[:], 0.0).then_inc(inis, 1)
                e.wait_ge(inis, 1)
                e.memset(kvbf[:], 0.0).then_inc(inis, 1)
                e.wait_ge(inis, 2)
                e.memset(k132[:], 0.0).then_inc(inis, 1)
                e.wait_ge(inis, 3)
                e.memset(k1bf[:], 0.0).then_inc(kvd, 1)

    _dump(p, "of1", p.of, [NCD, 128, S], BF16)


def _dump(p, name, src, shape, dtype):
    if name not in p.dbg:
        return
    dump = p.nc.dram_tensor(f"dbg_{name}", shape, dtype, kind="ExternalOutput")
    sd = p.sem("dump")
    with p.nc.Block() as b:
        @b.sync
        def _(e):
            e.dma_start(dump.ap(), src.ap()).then_inc(sd, 16)
            e.wait_ge(sd, 16)


def emit_ffna(p, l, xn_sb):
    nc = p.nc
    u = p.uid()
    w1ld = p.sem("fa_w1ld")
    w2ld = p.sem("fa_w2ld")
    mm1 = p.sem("fa_mm1")
    mm2 = p.sem("fa_mm2")
    sld = p.sem("fa_sld")
    gd = p.sem("fa_gd")
    std = p.sem("fa_std")
    s1v = p.vec[f"s1{l}"]
    s2v = p.vec[f"s2{l}"]
    w2cd = p.sem("fa_w2cd")
    w2ldp = [p.sem("fa_w2ld0"), p.sem("fa_w2ld1")]
    with (
        nc.sbuf_tensor(f"fa_w1{u}", [128, NCD * DFF], BF16) as w1sb,
        nc.sbuf_tensor(f"fa_w2{u}", [128, 2 * NCD * 128], BF16) as w2sb,
        nc.sbuf_tensor(f"fa_w2f{u}", [128, 2 * NCD * 128], F32) as w2f32,
        nc.sbuf_tensor(f"fa_t1{u}", [128, 2 * TG], F32) as t1,
        nc.sbuf_tensor(f"fa_st{u}", [128, 2 * S], BF16) as stg,
        nc.psum_tensor(f"fa_p1{u}", [128, 2 * TG], F32) as p1,
        nc.psum_tensor(f"fa_p2{u}", [128, 2 * TG], F32) as p2,
        nc.Block(f"fa{u}") as b,
    ):
        @b.gpsimd
        def _(e):
            e.dma_start(w1sb[:], p.wt[f"w1{l}"].ap().rearrange("p c o -> p (c o)"),
                        max_dma_last_dim=8192).then_inc(w1ld, 16)
            e.wait_ge(w1ld, 16)

        @b.tensor
        def _(e):
            e.wait_ge(w1ld, 16)
            un = 0
            for oc in range(NCF):
                e.wait_ge(w2cd, oc + 1)
                for tg in range(NTG):
                    un += 1
                    pa = p1[:, (un % 2) * TG:(un % 2) * TG + TG]
                    pb = p2[:, (un % 2) * TG:(un % 2) * TG + TG]
                    if un > 2:
                        e.wait_ge(sld, un - 2)
                    for ic in range(NCD):
                        mm = e.matmul(
                            pa,
                            w1sb[:, ic * DFF + oc * 128: ic * DFF + oc * 128 + 128],
                            xn_sb[:, ic * S + tg * TG: ic * S + (tg + 1) * TG],
                            start=(ic == 0), stop=(ic == NCD - 1))
                    mm.then_inc(mm1, 1)
                    if un > 2:
                        e.wait_ge(gd, un - 2)
                    for ic in range(NCD):
                        mm = e.matmul(
                            pb,
                            w2sb[:, (oc % 2) * 1024 + ic * 128:(oc % 2) * 1024 + ic * 128 + 128],
                            xn_sb[:, ic * S + tg * TG: ic * S + (tg + 1) * TG],
                            start=(ic == 0), stop=(ic == NCD - 1))
                    mm.then_inc(mm2, 1)

        @b.scalar
        def _(e):
            un = 0
            for oc in range(NCF):
                for tg in range(NTG):
                    un += 1
                    e.wait_ge(mm1, un)
                    if un > 2:
                        e.wait_ge(gd, un - 2)
                    pa = p1[:, (un % 2) * TG:(un % 2) * TG + TG]
                    e.activation(t1[:, (un % 2) * TG:(un % 2) * TG + TG], pa,
                                 AF.Silu, scale=s1v[:, oc:oc + 1]).then_inc(sld, 1)

        @b.vector
        def _(e):
            def w2cast(oc):
                e.wait_ge(w2ldp[oc % 2], 16 * (oc // 2 + 1))
                if oc >= 2:
                    e.wait_ge(mm2, (oc - 1) * NTG)
                e.tensor_copy(w2sb[:, (oc % 2) * 1024:(oc % 2) * 1024 + 1024],
                              w2f32[:, (oc % 2) * 1024:(oc % 2) * 1024 + 1024]
                              ).then_inc(w2cd, 1)

            w2cast(0)
            w2cast(1)
            un = 0
            for oc in range(NCF):
                if oc >= 2:
                    e.wait_ge(std, 16 * (oc - 1))
                for tg in range(NTG):
                    un += 1
                    e.wait_ge(mm2, un)
                    e.wait_ge(sld, un)
                    pb = p2[:, (un % 2) * TG:(un % 2) * TG + TG]
                    out = stg[:, (oc % 2) * S + tg * TG:(oc % 2) * S + (tg + 1) * TG]
                    e.scalar_tensor_tensor(
                        out, pb, s2v[:, oc:oc + 1],
                        t1[:, (un % 2) * TG:(un % 2) * TG + TG],
                        ALU.mult, ALU.mult).then_inc(gd, 1)
                if oc + 2 < NCF:
                    w2cast(oc + 2)

        @b.sync
        def _(e):
            def w2load(oc):
                if oc >= 2:
                    e.wait_ge(w2cd, oc - 1)
                e.dma_start(w2f32[:, (oc % 2) * 1024:(oc % 2) * 1024 + 1024],
                            p.wt[f"w2{l}"][oc, :, :]).then_inc(w2ldp[oc % 2], 16)

            w2load(0)
            w2load(1)
            for oc in range(NCF):
                if oc + 2 < NCF:
                    w2load(oc + 2)
                e.wait_ge(gd, (oc + 1) * NTG)
                e.dma_start(p.g[oc, :, :],
                            stg[:, (oc % 2) * S:(oc % 2) * S + S]).then_inc(std, 16)
                e.wait_ge(std, 16 * (oc + 1))

    _dump(p, f"g{l}", p.g, [NCF, 128, S], BF16)


def emit_ffnb(p, l):
    nc = p.nc
    u = p.uid()
    gld = p.sem("fb_gld")
    wld = p.sem("fb_wld")
    mmd = p.sem("fb_mmd")
    evd = p.sem("fb_evd")
    std = p.sem("fb_std")
    hs1 = p.vec[f"hs1{l}"]
    hs2 = p.vec[f"hs2{l}"]
    NGT = FT // TG
    wcd = p.sem("fb_wcd")
    wldp = [p.sem("fb_wld0"), p.sem("fb_wld1")]
    NU = NFT * 64

    def w_src(p_, un):
        ft, r = divmod(un, 64)
        st, oc = divmod(r, NCF)
        return (p_.wt[f"h1{l}"] if st == 0 else p_.wt[f"h2{l}"])[oc, :, :]

    with (
        nc.sbuf_tensor(f"fb_g{u}", [128, NCF * FT], BF16) as gt,
        nc.sbuf_tensor(f"fb_h{u}", [128, NCF * FT], BF16) as h1t,
        nc.sbuf_tensor(f"fb_w{u}", [128, 2 * DFF], BF16) as wsb,
        nc.sbuf_tensor(f"fb_wf{u}", [128, 2 * DFF], F32) as wf32,
        nc.sbuf_tensor(f"fb_st{u}", [128, 2 * FT], BF16) as stg,
        nc.psum_tensor(f"fb_ps{u}", [128, 2 * TG], F32) as ps,
        nc.Block(f"fb{u}") as b,
    ):
        @b.sync
        def _(e):
            def wload(un):
                if un >= 2:
                    e.wait_ge(wcd, un - 1)
                e.dma_start(wf32[:, (un % 2) * DFF:(un % 2) * DFF + DFF],
                            w_src(p, un)).then_inc(wldp[un % 2], 16)

            for ft in range(NFT):
                if ft > 0:
                    e.wait_ge(mmd, (ft * 64 - 32) * NGT)
                e.dma_start(
                    gt[:].rearrange("p (c t) -> p c t", c=NCF),
                    p.g[:, :, ft * FT:(ft + 1) * FT].rearrange("c p t -> p c t"),
                ).then_inc(gld, 16)
                e.wait_ge(gld, 16 * (ft + 1))
                if ft == 0:
                    wload(0)
                    wload(1)
                for st in range(2):
                    for oc in range(NCF):
                        un = ft * 64 + st * 32 + oc
                        if un + 2 < NU:
                            wload(un + 2)
                        if st == 1:
                            e.wait_ge(evd, un * NGT + NGT)
                            e.dma_start(p.h2a[oc, :, ft * FT:(ft + 1) * FT],
                                        stg[:, (oc % 2) * FT:(oc % 2) * FT + FT]
                                        ).then_inc(std, 16)
                            e.wait_ge(std, 16 * (ft * NCF + oc + 1))

        @b.vector
        def _(e):
            for un in range(NU):
                e.wait_ge(wldp[un % 2], 16 * (un // 2 + 1))
                if un >= 2:
                    e.wait_ge(mmd, (un - 1) * NGT)
                e.tensor_copy(wsb[:, (un % 2) * DFF:(un % 2) * DFF + DFF],
                              wf32[:, (un % 2) * DFF:(un % 2) * DFF + DFF]
                              ).then_inc(wcd, 1)

        @b.tensor
        def _(e):
            for ft in range(NFT):
                e.wait_ge(gld, 16 * (ft + 1))
                for st in range(2):
                    if st == 1:
                        e.wait_ge(evd, (ft * 64 + 32) * NGT)
                    for oc in range(NCF):
                        un = ft * 64 + st * 32 + oc
                        e.wait_ge(wcd, un + 1)
                        src = gt if st == 0 else h1t
                        for g in range(NGT):
                            gu = un * NGT + g
                            if gu >= 2:
                                e.wait_ge(evd, gu - 1)
                            pv = ps[:, (gu % 2) * TG:(gu % 2) * TG + TG]
                            for ic in range(NCF):
                                mm = e.matmul(
                                    pv,
                                    wsb[:, (un % 2) * DFF + ic * 128:(un % 2) * DFF + ic * 128 + 128],
                                    src[:, ic * FT + g * TG: ic * FT + g * TG + TG],
                                    start=(ic == 0), stop=(ic == NCF - 1))
                            mm.then_inc(mmd, 1)

        @b.scalar
        def _(e):
            for ft in range(NFT):
                for st in range(2):
                    for oc in range(NCF):
                        un = ft * 64 + st * 32 + oc
                        sv = hs1 if st == 0 else hs2
                        for g in range(NGT):
                            gu = un * NGT + g
                            e.wait_ge(mmd, gu + 1)
                            if st == 0:
                                out = h1t[:, oc * FT + g * TG: oc * FT + g * TG + TG]
                            else:
                                if g == 0 and oc >= 2:
                                    e.wait_ge(std, 16 * (ft * NCF + oc - 1))
                                out = stg[:, (oc % 2) * FT + g * TG:(oc % 2) * FT + g * TG + TG]
                            pv = ps[:, (gu % 2) * TG:(gu % 2) * TG + TG]
                            e.activation(out, pv, AF.Gelu,
                                         scale=sv[:, oc:oc + 1]).then_inc(evd, 1)

    _dump(p, f"h2a{l}", p.h2a, [NCF, 128, S], BF16)


def emit_c(p, l, last):
    nc = p.nc
    u = p.uid()
    wold = p.sem("c_wold")
    w3ld = p.sem("c_w3ld")
    hld = p.sem("c_hld")
    old_ = p.sem("c_old")
    xld = p.sem("c_xld")
    mm3 = p.sem("c_mm3")
    mmo = p.sem("c_mmo")
    tw = p.sem("c_tw")
    t2d = p.sem("c_t2d")
    pld = p.sem("c_pld")
    w3cd = p.sem("c_w3cd")
    w3ldp = [p.sem("c_w3ld0"), p.sem("c_w3ld1")]
    std = p.sem("c_std")
    xsrc = p.x if l == 0 else p.x1
    xdst = p.x1 if l == 0 else p.y
    s3v = p.vec[f"s3{l}"]
    bov = p.vec[f"bo{l}"]
    NGT = FT // TG
    with (
        nc.sbuf_tensor(f"c_h2{u}", [128, NCF * FT], BF16) as h2t,
        nc.sbuf_tensor(f"c_of{u}", [128, NCD * FT], BF16) as oft,
        nc.sbuf_tensor(f"c_x{u}", [128, NCD * FT], F32) as xt,
        nc.sbuf_tensor(f"c_wo{u}", [128, NCD * D], BF16) as wosb,
        nc.sbuf_tensor(f"c_w3{u}", [128, 2 * DFF], BF16) as w3sb,
        nc.sbuf_tensor(f"c_w3f{u}", [128, 2 * DFF], F32) as w3f32,
        nc.sbuf_tensor(f"c_two{u}", [128, 2 * TG], F32) as two,
        nc.sbuf_tensor(f"c_t2{u}", [128, 2 * TG], F32) as t2,
        nc.sbuf_tensor(f"c_st{u}", [128, 2 * FT], F32) as stg,
        nc.psum_tensor(f"c_p3{u}", [128, 2 * TG], F32) as p3,
        nc.psum_tensor(f"c_po{u}", [128, 2 * TG], F32) as po,
        nc.Block(f"c{u}") as b,
    ):
        @b.gpsimd
        def _(e):
            e.dma_start(wosb[:], p.wt[f"wo{l}"].ap().rearrange("p c o -> p (c o)"),
                        max_dma_last_dim=8192).then_inc(wold, 16)
            e.wait_ge(wold, 16)
            for ft in range(NFT):
                e.wait_ge(xld, 16 * (ft + 1))
                for oc in range(NCD):
                    un = ft * NCD + oc
                    for g in range(NGT):
                        gu = un * NGT + g
                        e.wait_ge(tw, gu + 1)
                        if gu >= 2:
                            e.wait_ge(t2d, gu - 1)
                        e.tensor_add(
                            t2[:, (gu % 2) * TG:(gu % 2) * TG + TG],
                            two[:, (gu % 2) * TG:(gu % 2) * TG + TG],
                            xt[:, oc * FT + g * TG: oc * FT + g * TG + TG]
                        ).then_inc(pld, 1)

        @b.sync
        def _(e):
            def w3load(un):
                if un >= 2:
                    e.wait_ge(w3cd, un - 1)
                e.dma_start(w3f32[:, (un % 2) * DFF:(un % 2) * DFF + DFF],
                            p.wt[f"w3{l}"][un % NCD, :, :]).then_inc(w3ldp[un % 2], 16)

            for ft in range(NFT):
                if ft > 0:
                    e.wait_ge(mm3, ft * NCD * NGT)
                    e.wait_ge(mmo, ft * NCD * NGT)
                    e.wait_ge(pld, ft * NCD * NGT)
                sl = slice(ft * FT, (ft + 1) * FT)
                e.dma_start(h2t[:].rearrange("p (c t) -> p c t", c=NCF),
                            p.h2a[:, :, sl].rearrange("c p t -> p c t")
                            ).then_inc(hld, 16)
                e.wait_ge(hld, 16 * (ft + 1))
                e.dma_start(oft[:].rearrange("p (c t) -> p c t", c=NCD),
                            p.of[:, :, sl].rearrange("c p t -> p c t")
                            ).then_inc(old_, 16)
                e.wait_ge(old_, 16 * (ft + 1))
                e.dma_start(xt[:].rearrange("p (c t) -> p c t", c=NCD),
                            xsrc[:, :, sl].rearrange("c p t -> p c t")
                            ).then_inc(xld, 16)
                e.wait_ge(xld, 16 * (ft + 1))
                if ft == 0:
                    w3load(0)
                    w3load(1)
                for oc in range(NCD):
                    un = ft * NCD + oc
                    if un + 2 < NFT * NCD:
                        w3load(un + 2)
                    e.wait_ge(t2d, un * NGT + NGT)
                    e.dma_start(xdst[oc, :, sl],
                                stg[:, (oc % 2) * FT:(oc % 2) * FT + FT]
                                ).then_inc(std, 16)
                    e.wait_ge(std, 16 * (ft * NCD + oc + 1))

        @b.tensor
        def _(e):
            e.wait_ge(wold, 16)
            for ft in range(NFT):
                e.wait_ge(hld, 16 * (ft + 1))
                e.wait_ge(old_, 16 * (ft + 1))
                for oc in range(NCD):
                    un = ft * NCD + oc
                    e.wait_ge(w3cd, un + 1)
                    for g in range(NGT):
                        gu = un * NGT + g
                        if gu >= 2:
                            e.wait_ge(t2d, gu - 1)
                            e.wait_ge(tw, gu - 1)
                        pv3 = p3[:, (gu % 2) * TG:(gu % 2) * TG + TG]
                        pvo = po[:, (gu % 2) * TG:(gu % 2) * TG + TG]
                        for ic in range(NCF):
                            mm = e.matmul(
                                pv3,
                                w3sb[:, (un % 2) * DFF + ic * 128:(un % 2) * DFF + ic * 128 + 128],
                                h2t[:, ic * FT + g * TG: ic * FT + g * TG + TG],
                                start=(ic == 0), stop=(ic == NCF - 1))
                        mm.then_inc(mm3, 1)
                        for ic in range(NCD):
                            mm = e.matmul(
                                pvo,
                                wosb[:, ic * D + oc * 128: ic * D + oc * 128 + 128],
                                oft[:, ic * FT + g * TG: ic * FT + g * TG + TG],
                                start=(ic == 0), stop=(ic == NCD - 1))
                        mm.then_inc(mmo, 1)

        @b.scalar
        def _(e):
            for ft in range(NFT):
                for oc in range(NCD):
                    for g in range(NGT):
                        gu = (ft * NCD + oc) * NGT + g
                        e.wait_ge(mmo, gu + 1)
                        if gu >= 2:
                            e.wait_ge(pld, gu - 1)
                        pvo = po[:, (gu % 2) * TG:(gu % 2) * TG + TG]
                        e.activation(two[:, (gu % 2) * TG:(gu % 2) * TG + TG],
                                     pvo, AF.Identity,
                                     bias=bov[:, oc:oc + 1]).then_inc(tw, 1)

        @b.vector
        def _(e):
            def w3cast(un):
                e.wait_ge(w3ldp[un % 2], 16 * (un // 2 + 1))
                if un >= 2:
                    e.wait_ge(mm3, (un - 1) * NGT)
                e.tensor_copy(w3sb[:, (un % 2) * DFF:(un % 2) * DFF + DFF],
                              w3f32[:, (un % 2) * DFF:(un % 2) * DFF + DFF]
                              ).then_inc(w3cd, 1)

            w3cast(0)
            w3cast(1)
            for ft in range(NFT):
                for oc in range(NCD):
                    un = ft * NCD + oc
                    if un + 2 < NFT * NCD:
                        w3cast(un + 2)
                    if oc >= 2:
                        e.wait_ge(std, 16 * (ft * NCD + oc - 1))
                    for g in range(NGT):
                        gu = un * NGT + g
                        e.wait_ge(mm3, gu + 1)
                        e.wait_ge(pld, gu + 1)
                        pv3 = p3[:, (gu % 2) * TG:(gu % 2) * TG + TG]
                        e.scalar_tensor_tensor(
                            stg[:, (oc % 2) * FT + g * TG:(oc % 2) * FT + g * TG + TG],
                            pv3, s3v[:, oc:oc + 1],
                            t2[:, (gu % 2) * TG:(gu % 2) * TG + TG],
                            ALU.mult, ALU.add).then_inc(t2d, 1)

    if last and f"x1_{l}" in p.dbg:
        pass
    if not last:
        _dump(p, f"x1_{l}", p.x1, [NCD, 128, S], F32)


def emit_zero_of(p):
    nc = p.nc
    u = p.uid()
    sd = p.sem("zof")
    sz = p.sem("zofc")
    with (
        nc.sbuf_tensor(f"zof{u}", [128, S], BF16) as z,
        nc.Block(f"zof{u}") as b,
    ):
        @b.gpsimd
        def _(e):
            e.memset(z[:], 0.0).then_inc(sz, 1)

        @b.sync
        def _(e):
            e.wait_ge(sz, 1)
            for oc in range(NCD):
                e.dma_start(p.of[oc, :, :], z[:]).then_inc(sd, 16)
                e.wait_ge(sd, 16 * (oc + 1))


def build_program(dbg=None, upto=99):
    nc = bass.Bass("TRN2", target_bir_lowering=False)
    p = P(nc, dbg)
    declare_io(p)
    emit_const(p)
    ph = upto if isinstance(upto, set) else None
    if ph is None:
        ph = set()
        if upto >= 1:
            ph |= {"ln"}
        if upto >= 2:
            ph |= {"qkv"}
        if upto >= 3:
            ph |= {"ffna"}
        if upto >= 4:
            ph |= {"ffnb"}
        if upto >= 5:
            ph |= {"c"}
        if upto >= 6:
            ph |= {"l1"}
        if upto >= 7:
            ph |= {"attn"}
    if "zero_of" in p.dbg:
        emit_zero_of(p)
    p.recycle()
    for l in range(2):
        with nc.sbuf_tensor(f"xn{l}", [128, NCD * S], BF16) as xn_sb:
            emit_ln(p, l, xn_sb)
            p.recycle()
            if "qkv" in ph:
                emit_qkv(p, l, xn_sb)
            if "ffna" in ph:
                emit_ffna(p, l, xn_sb)
                p.recycle()
        if "attn" in ph:
            if l == 0:
                emit_attn0(p)
            else:
                emit_attn1(p)
            p.recycle()
        if "ffnb" in ph:
            emit_ffnb(p, l)
            p.recycle()
        if "c" in ph:
            emit_c(p, l, last=(l == 1))
            p.recycle()
        if "l1" not in ph:
            break
    return nc, p


def kernel(**inputs):
    w = _prep_weights(inputs)
    x = np.asarray(inputs["L0_attn_x"], dtype=np.float32)
    in_maps = []
    for b in range(8):
        m = dict(w)
        m["x"] = np.ascontiguousarray(x[b].T).reshape(NCD, 128, S)
        in_maps.append(m)
    nc, p = build_program()
    res = run_bass_kernel_spmd(nc, in_maps, core_ids=list(range(8)))
    out = np.stack([np.ascontiguousarray(
        np.asarray(res.results[b]["y"]).reshape(D, S).T) for b in range(8)])
    return np.ascontiguousarray(out, dtype=np.float32)


if __name__ == "__main__":
    nc, p = build_program(dbg={"xn0"}, upto=1)
    print("built ok")


# version 05 (final)
# speedup vs baseline: 1.4253x; 1.4253x over previous
import numpy as np
import concourse.bass as bass
import concourse.mybir as mybir
from concourse.bass_utils import run_bass_kernel_spmd

F32 = mybir.dt.float32
BF16 = mybir.dt.bfloat16
AF = mybir.ActivationFunctionType
ALU = mybir.AluOpType
AX = mybir.AxisListType

D = 1024
S = 4096
H = 16
HD = 64
DFF = 4096
WIN = 1024
NCD = D // 128
NCF = DFF // 128
NQC = S // 128
TG = 512
NTG = S // TG
FT = 1024
NFT = S // FT

LN_EPS = 1e-5
NEG_BIG = -1.0e9



def _blk_wT_full(w):
    wT = np.ascontiguousarray(np.transpose(w))
    nin, nout = wT.shape
    return np.ascontiguousarray(wT.reshape(nin // 128, 128, nout).transpose(1, 0, 2))


def _blk_w_oc(w, ocs=128):
    wT = np.ascontiguousarray(np.transpose(w))
    nin, nout = wT.shape
    a = wT.reshape(nin // 128, 128, nout // ocs, ocs).transpose(2, 1, 0, 3)
    return np.ascontiguousarray(a.reshape(nout // ocs, 128, (nin // 128) * ocs))


def _blk_vec(v):
    n = v.shape[0]
    return np.ascontiguousarray(v.reshape(n // 128, 128).T)


def _prep_weights(inp):
    w = {}
    for l in range(2):
        if l == 0:
            aw = {k: inp[f"L0_attn_{k}"] for k in
                  ["wq", "bq", "wk", "bk", "wv", "bv", "wo", "bo"]}
        else:
            aw = {"wq": inp["L1_attn_Wq"], "bq": inp["L1_attn_bq"],
                  "wk": inp["L1_attn_Wk"], "bk": inp["L1_attn_bk"],
                  "wv": inp["L1_attn_Wv"], "bv": inp["L1_attn_bv"],
                  "wo": inp["L1_attn_Wo"], "bo": inp["L1_attn_bo"]}
        f = {k: inp[f"L{l}_ffn_{k}"] for k in
             ["w1_w", "w1_s", "w2_w", "w2_s", "h1_w", "h1_s", "h2_w", "h2_s",
              "w3_w", "w3_s"]}
        w[f"wq{l}"] = _blk_wT_full(aw["wq"])
        w[f"wk{l}"] = _blk_wT_full(aw["wk"])
        w[f"wv{l}"] = _blk_wT_full(aw["wv"])
        w[f"wo{l}"] = _blk_wT_full(aw["wo"])
        w[f"bq{l}"] = _blk_vec(aw["bq"])
        w[f"bk{l}"] = _blk_vec(aw["bk"])
        w[f"bv{l}"] = _blk_vec(aw["bv"])
        w[f"bo{l}"] = _blk_vec(aw["bo"])
        w[f"w1{l}"] = _blk_wT_full(f["w1_w"])
        w[f"w2{l}"] = _blk_w_oc(f["w2_w"])
        w[f"h1{l}"] = _blk_w_oc(f["h1_w"])
        w[f"h2{l}"] = _blk_w_oc(f["h2_w"])
        w[f"w3{l}"] = _blk_w_oc(f["w3_w"])
        w[f"s1{l}"] = _blk_vec(f["w1_s"])
        w[f"s2{l}"] = _blk_vec(f["w2_s"])
        w[f"hs1{l}"] = _blk_vec(f["h1_s"])
        w[f"hs2{l}"] = _blk_vec(f["h2_s"])
        w[f"s3{l}"] = _blk_vec(f["w3_s"])
    return {k: np.ascontiguousarray(v, dtype=np.float32) for k, v in w.items()}



class P:

    def __init__(self, nc, dbg=None):
        self.nc = nc
        self.dbg = dbg or set()
        self.sem_i = 0
        self.phase_sems = []

    def sem(self, name):
        self.sem_i += 1
        h = self.nc.alloc_semaphore(f"{name}_{self.sem_i}")
        self.phase_sems.append(h)
        return h

    def recycle(self):
        if self.phase_sems:
            self.nc.clear_and_free_semaphores(self.phase_sems)
            self.nc.all_engine_barrier()
            self.phase_sems = []

    def uid(self):
        self.sem_i += 1
        return self.sem_i


def declare_io(p):
    nc = p.nc
    p.x = nc.dram_tensor("x", [NCD, 128, S], F32, kind="ExternalInput")
    p.y = nc.dram_tensor("y", [NCD, 128, S], F32, kind="ExternalOutput")
    p.wt = {}
    for l in range(2):
        for k in ["wq", "wk", "wv", "wo"]:
            p.wt[f"{k}{l}"] = nc.dram_tensor(f"{k}{l}", [128, NCD, D], F32,
                                             kind="ExternalInput")
        for k in ["bq", "bk", "bv", "bo"]:
            p.wt[f"{k}{l}"] = nc.dram_tensor(f"{k}{l}", [128, NCD], F32,
                                             kind="ExternalInput")
        p.wt[f"w1{l}"] = nc.dram_tensor(f"w1{l}", [128, NCD, DFF], F32,
                                        kind="ExternalInput")
        p.wt[f"w2{l}"] = nc.dram_tensor(f"w2{l}", [NCF, 128, NCD * 128], F32,
                                        kind="ExternalInput")
        p.wt[f"h1{l}"] = nc.dram_tensor(f"h1{l}", [NCF, 128, DFF], F32,
                                        kind="ExternalInput")
        p.wt[f"h2{l}"] = nc.dram_tensor(f"h2{l}", [NCF, 128, DFF], F32,
                                        kind="ExternalInput")
        p.wt[f"w3{l}"] = nc.dram_tensor(f"w3{l}", [NCD, 128, DFF], F32,
                                        kind="ExternalInput")
        for k in ["s1", "s2", "hs1", "hs2"]:
            p.wt[f"{k}{l}"] = nc.dram_tensor(f"{k}{l}", [128, NCF], F32,
                                             kind="ExternalInput")
        p.wt[f"s3{l}"] = nc.dram_tensor(f"s3{l}", [128, NCD], F32,
                                        kind="ExternalInput")
    p.x1 = nc.dram_tensor("x1", [NCD, 128, S], F32, kind="Internal")
    p.qf = nc.dram_tensor("qf", [NCD, 128, S], BF16, kind="Internal")
    p.kf = nc.dram_tensor("kf", [NCD, 128, S], BF16, kind="Internal")
    p.vf = nc.dram_tensor("vf", [NCD, 128, S], BF16, kind="Internal")
    p.of = nc.dram_tensor("of", [NCD, 128, S], BF16, kind="Internal")
    p.g = nc.dram_tensor("g", [NCF, 128, S], BF16, kind="Internal")
    p.h2a = nc.dram_tensor("h2a", [NCF, 128, S], BF16, kind="Internal")


def emit_const(p):
    nc = p.nc
    p.ones_f = nc.alloc_sbuf_tensor("ones_f", [128, 1], F32)
    p.ones_b = nc.alloc_sbuf_tensor("ones_b", [128, 1], BF16)
    p.onesr_f = nc.alloc_sbuf_tensor("onesr_f", [1, 128], F32)
    p.ident_b = nc.alloc_sbuf_tensor("ident_b", [128, 128], BF16)
    p.b0 = nc.alloc_sbuf_tensor("b0", [128, 1152], F32)
    p.ones4 = nc.alloc_sbuf_tensor("ones4", [128, 128], F32)
    p.mask16 = nc.alloc_sbuf_tensor("mask16", [128, H * 128], BF16)
    p.vec = {}
    for l in range(2):
        for k, w in [("bq", NCD), ("bk", NCD), ("bv", NCD), ("bo", NCD),
                     ("s1", NCF), ("s2", NCF), ("hs1", NCF), ("hs2", NCF),
                     ("s3", NCD)]:
            p.vec[f"{k}{l}"] = nc.alloc_sbuf_tensor(f"v_{k}{l}", [128, w], F32)
    p.bk0s = nc.alloc_sbuf_tensor("bk0s", [128, NCD], F32)

    ld = p.sem("c_ld")
    cs = p.sem("c_cs")
    n_ld = 0
    with nc.Block("const") as b:
        @b.sync
        def _(e):
            nonlocal n_ld
            for l in range(2):
                for k in ["bq", "bk", "bv", "bo", "s1", "s2", "hs1", "hs2", "s3"]:
                    e.dma_start(p.vec[f"{k}{l}"][:], p.wt[f"{k}{l}"].ap()
                                ).then_inc(ld, 16)
                    n_ld += 16

        @b.gpsimd
        def _(e):
            n = 0

            def step(inst):
                nonlocal n
                n += 1
                inst.then_inc(cs, 1)
                e.wait_ge(cs, n)

            step(e.memset(p.ones_f[:], 1.0))
            step(e.memset(p.ones_b[:], 1.0))
            step(e.memset(p.onesr_f[:], 1.0))
            step(e.memset(p.ones4[:], 1.0))
            step(e.memset(p.ident_b[:], 1.0))
            step(e.affine_select(out=p.ident_b[:], in_=p.ident_b[:],
                                 compare_op=ALU.is_ge, fill=0.0,
                                 base=0, pattern=[[-1, 128]], channel_multiplier=1))
            step(e.affine_select(out=p.ident_b[:], in_=p.ident_b[:],
                                 compare_op=ALU.is_ge, fill=0.0,
                                 base=0, pattern=[[1, 128]], channel_multiplier=-1))
            step(e.iota(p.b0[:], [[1, 1152]], base=-1024, channel_multiplier=-1,
                        allow_small_or_imprecise_dtypes=True))
            step(e.affine_select(out=p.b0[:], in_=p.b0[:], compare_op=ALU.is_ge,
                                 fill=NEG_BIG, base=1024, pattern=[[-1, 1152]],
                                 channel_multiplier=1))
            step(e.affine_select(out=p.b0[:], in_=p.b0[:], compare_op=ALU.is_ge,
                                 fill=NEG_BIG, base=-1, pattern=[[1, 1152]],
                                 channel_multiplier=-1))
            for hh in range(H):
                sl = p.mask16[:, hh * 128:(hh + 1) * 128]
                step(e.memset(sl, 1.0))
                step(e.affine_select(out=sl, in_=sl, compare_op=ALU.is_ge,
                                     fill=0.0, base=0, pattern=[[1, 128]],
                                     channel_multiplier=-1))

        @b.vector
        def _(e):
            e.wait_ge(ld, n_ld)
            e.tensor_scalar_mul(p.bk0s[:], p.vec["bk0"][:], 0.125)


def emit_ln(p, l, xn_sb):
    nc = p.nc
    src = p.x if l == 0 else p.x1
    ld = p.sem("ln_ld")
    sq = p.sem("ln_sq")
    s1d = p.sem("ln_s1")
    s2d = p.sem("ln_s2")
    mvar = p.sem("ln_mvar")
    rd = p.sem("ln_rd")
    dch = p.sem("ln_dch")
    bcd = p.sem("ln_bcd")
    xnd = p.sem("ln_xnd")

    u = p.uid()
    with (
        nc.sbuf_tensor(f"ln_xsb{u}", [128, NCD * TG], F32) as xsb,
        nc.sbuf_tensor(f"ln_sq{u}", [128, NCD * TG], F32) as sq_sb,
        nc.sbuf_tensor(f"ln_m{u}", [1, TG], F32) as m_sb,
        nc.sbuf_tensor(f"ln_e2{u}", [1, TG], F32) as e2_sb,
        nc.sbuf_tensor(f"ln_msq{u}", [1, TG], F32) as msq_sb,
        nc.sbuf_tensor(f"ln_var{u}", [1, TG], F32) as var_sb,
        nc.sbuf_tensor(f"ln_lnv{u}", [1, TG], F32) as lnv_sb,
        nc.sbuf_tensor(f"ln_r{u}", [1, TG], F32) as r_sb,
        nc.psum_tensor(f"ln_ps1{u}", [1, TG], F32) as ps1,
        nc.psum_tensor(f"ln_ps2{u}", [1, TG], F32) as ps2,
        nc.psum_tensor(f"ln_bcm{u}", [128, TG], F32) as bcm,
        nc.psum_tensor(f"ln_bcr{u}", [128, TG], F32) as bcr,
        nc.Block(f"ln{u}") as b,
    ):
        @b.sync
        def _(e):
            for t in range(NTG):
                if t > 0:
                    e.wait_ge(xnd, t)
                e.dma_start(
                    xsb[:].rearrange("p (c t) -> p c t", c=NCD),
                    src[:, :, t * TG:(t + 1) * TG].rearrange("c p t -> p c t"),
                ).then_inc(ld, 16)

        @b.scalar
        def _(e):
            for t in range(NTG):
                e.wait_ge(ld, 16 * (t + 1))
                e.activation(sq_sb[:], xsb[:], AF.Square).then_inc(sq, 1)
                e.wait_ge(mvar, t + 1)
                e.activation(lnv_sb[:], var_sb[:], AF.Ln).then_inc(rd, 1)
                e.wait_ge(rd, 2 * t + 1)
                e.activation(r_sb[:], lnv_sb[:], AF.Exp, scale=-0.5).then_inc(rd, 1)

        @b.tensor
        def _(e):
            for t in range(NTG):
                e.wait_ge(ld, 16 * (t + 1))
                if t > 0:
                    e.wait_ge(mvar, t)
                for c in range(NCD):
                    mm = e.matmul(ps1[:], p.ones_f[:], xsb[:, c * TG:(c + 1) * TG],
                                  start=(c == 0), stop=(c == NCD - 1))
                mm.then_inc(s1d, 1)
                e.wait_ge(sq, t + 1)
                for c in range(NCD):
                    mm = e.matmul(ps2[:], p.ones_f[:], sq_sb[:, c * TG:(c + 1) * TG],
                                  start=(c == 0), stop=(c == NCD - 1))
                mm.then_inc(s2d, 1)
                e.wait_ge(rd, 2 * (t + 1))
                if t > 0:
                    e.wait_ge(xnd, t)
                e.matmul(bcm[:], p.onesr_f[:], m_sb[:], start=True, stop=True)
                e.matmul(bcr[:], p.onesr_f[:], r_sb[:], start=True, stop=True
                         ).then_inc(bcd, 1)

        @b.vector
        def _(e):
            nch = 0
            for t in range(NTG):
                e.wait_ge(s1d, t + 1)
                e.tensor_scalar_mul(m_sb[:], ps1[:], 1.0 / D).then_inc(dch, 1)
                nch += 1
                e.wait_ge(s2d, t + 1)
                e.tensor_scalar(e2_sb[:], ps2[:], 1.0 / D, LN_EPS,
                                ALU.mult, ALU.add).then_inc(dch, 1)
                nch += 1
                e.wait_ge(dch, nch)
                e.tensor_mul(msq_sb[:], m_sb[:], m_sb[:]).then_inc(dch, 1)
                nch += 1
                e.wait_ge(dch, nch)
                e.tensor_sub(var_sb[:], e2_sb[:], msq_sb[:]).then_inc(mvar, 1)
                e.wait_ge(bcd, t + 1)
                for c in range(NCD):
                    cs = xsb[:, c * TG:(c + 1) * TG]
                    dst = xn_sb[:, c * S + t * TG: c * S + (t + 1) * TG]
                    e.tensor_sub(sq_sb[:, c * TG:(c + 1) * TG], cs, bcm[:]
                                 ).then_inc(dch, 1)
                    nch += 1
                    e.wait_ge(dch, nch)
                    tt = e.tensor_mul(dst, sq_sb[:, c * TG:(c + 1) * TG], bcr[:])
                tt.then_inc(xnd, 1)

    if f"xn{l}" in p.dbg:
        dump = nc.dram_tensor(f"dbg_xn{l}", [NCD, 128, S], BF16,
                              kind="ExternalOutput")
        sd = p.sem("ln_dump")
        with nc.Block(f"ln_dump{l}") as b:
            @b.sync
            def _(e):
                e.dma_start(dump.ap().rearrange("c p t -> p c t"),
                            xn_sb[:].rearrange("p (c t) -> p c t", c=NCD)
                            ).then_inc(sd, 16)
                e.wait_ge(sd, 16)


def emit_proj(p, l, wname, bname, dst, xn_sb, mode):
    nc = p.nc
    u = p.uid()
    wld = p.sem("pj_wld")
    mmd = p.sem("pj_mmd")
    evd = p.sem("pj_evd")
    std = p.sem("pj_std")
    bias = p.bk0s if mode == "scale8" else p.vec[f"{bname}{l}"]
    with (
        nc.sbuf_tensor(f"pj_w{u}", [128, NCD * D], BF16) as wsb,
        nc.sbuf_tensor(f"pj_st{u}", [128, 2 * S], BF16) as stg,
        nc.psum_tensor(f"pj_ps{u}", [128, 2 * TG], F32) as ps,
        nc.Block(f"pj{u}") as b,
    ):
        @b.gpsimd
        def _(e):
            e.dma_start(wsb[:], p.wt[f"{wname}{l}"].ap().rearrange("p c o -> p (c o)"),
                        max_dma_last_dim=8192).then_inc(wld, 16)

        @b.tensor
        def _(e):
            e.wait_ge(wld, 16)
            un = 0
            for oc in range(NCD):
                for tg in range(NTG):
                    un += 1
                    if un > 2:
                        e.wait_ge(evd, un - 2)
                    pv = ps[:, (un % 2) * TG:(un % 2) * TG + TG]
                    for ic in range(NCD):
                        mm = e.matmul(
                            pv,
                            wsb[:, ic * D + oc * 128: ic * D + oc * 128 + 128],
                            xn_sb[:, ic * S + tg * TG: ic * S + (tg + 1) * TG],
                            start=(ic == 0), stop=(ic == NCD - 1))
                    mm.then_inc(mmd, 1)

        @b.scalar
        def _(e):
            un = 0
            for oc in range(NCD):
                if oc >= 2:
                    e.wait_ge(std, 16 * (oc - 1))
                for tg in range(NTG):
                    un += 1
                    e.wait_ge(mmd, un)
                    pv = ps[:, (un % 2) * TG:(un % 2) * TG + TG]
                    out = stg[:, (oc % 2) * S + tg * TG:(oc % 2) * S + (tg + 1) * TG]
                    bsl = bias[:, oc:oc + 1]
                    if mode == "exp":
                        a = e.activation(out, pv, AF.Exp, bias=bsl)
                    elif mode == "scale8":
                        a = e.activation(out, pv, AF.Identity, bias=bsl, scale=0.125)
                    else:
                        a = e.activation(out, pv, AF.Identity, bias=bsl)
                    a.then_inc(evd, 1)

        @b.sync
        def _(e):
            for oc in range(NCD):
                e.wait_ge(evd, oc * NTG + NTG)
                e.dma_start(dst[oc, :, :], stg[:, (oc % 2) * S:(oc % 2) * S + S]).then_inc(std, 16)
                e.wait_ge(std, 16 * (oc + 1))


def emit_qkv(p, l, xn_sb):
    if l == 0:
        specs = [("wq", "bq", p.qf, "id"), ("wk", "bk", p.kf, "scale8"),
                 ("wv", "bv", p.vf, "id")]
    else:
        specs = [("wq", "bq", p.qf, "exp"), ("wk", "bk", p.kf, "exp"),
                 ("wv", "bv", p.vf, "id")]
    for wn, bn, dst, mode in specs:
        emit_proj(p, l, wn, bn, dst, xn_sb, mode)
        p.recycle()
    for nm, t in [("qf", p.qf), ("kf", p.kf), ("vf", p.vf)]:
        if f"{nm}{l}" in p.dbg:
            dump = p.nc.dram_tensor(f"dbg_{nm}{l}", [NCD, 128, S], BF16,
                                    kind="ExternalOutput")
            sd = p.sem("qkv_dump")
            with p.nc.Block() as b:
                @b.sync
                def _(e):
                    e.dma_start(dump.ap(), t.ap()).then_inc(sd, 16)
                    e.wait_ge(sd, 16)


def emit_attn0(p):
    import math
    nc = p.nc
    u = p.uid()
    slopes = []
    start = 2 ** (-2 ** (-(math.log2(H) - 3)))
    for i in range(H):
        slopes.append(float(start * start ** i))

    qld = p.sem("a0_qld")
    kld = p.sem("a0_kld")
    vld = p.sem("a0_vld")
    vtd = p.sem("a0_vtd")
    mmS = p.sem("a0_mmS")
    biasd = p.sem("a0_bias")
    maxd = p.sem("a0_maxd")
    expd = p.sem("a0_expd")
    recd = p.sem("a0_recd")
    normd = p.sem("a0_norm")
    pta = p.sem("a0_pta")
    ptev = p.sem("a0_ptev")
    pvd = p.sem("a0_pvd")
    oev = p.sem("a0_oev")
    ofd = p.sem("a0_ofd")

    RING = 10

    def nkc(qc):
        return min(qc + 1, 9)

    pta_cum = [0]
    ptev_cum_head = []
    ptev_cum_qc = [0]
    ptev_c = 0
    pta_c = 0
    for qc in range(NQC):
        rounds = 1 + (1 if nkc(qc) > 8 else 0)
        ptev_c += 1
        for h in range(H):
            pta_c += rounds
            pta_cum.append(pta_c)
            ptev_c += rounds
            ptev_cum_head.append(ptev_c)
        ptev_cum_qc.append(ptev_c)

    with (
        nc.sbuf_tensor(f"a0_q{u}", [128, 2 * D], BF16) as q_sb,
        nc.sbuf_tensor(f"a0_vs{u}", [128, 2 * D], BF16) as vs,
        nc.sbuf_tensor(f"a0_kr{u}", [128, NCD * RING * 128], BF16) as k_ring,
        nc.sbuf_tensor(f"a0_vtr{u}", [128, RING * H * HD], BF16) as vt_ring,
        nc.sbuf_tensor(f"a0_ssb{u}", [128, 2 * 1152], F32) as s_sb,
        nc.sbuf_tensor(f"a0_psb{u}", [128, 2 * 1152], BF16) as p_sb,
        nc.sbuf_tensor(f"a0_pnb{u}", [128, 2 * 1152], BF16) as pn_sb,
        nc.sbuf_tensor(f"a0_ptsb{u}", [128, 1152], BF16) as pt_sb,
        nc.sbuf_tensor(f"a0_negm{u}", [128, 2], F32) as negm,
        nc.sbuf_tensor(f"a0_sum{u}", [128, 2], F32) as sums,
        nc.sbuf_tensor(f"a0_rin{u}", [128, 2], F32) as rinv,
        nc.sbuf_tensor(f"a0_ofs{u}", [128, 2 * D], BF16) as ofs,
        nc.psum_tensor(f"a0_sa{u}", [128, 1152], F32) as s_psA,
        nc.psum_tensor(f"a0_sb{u}", [128, 1152], F32) as s_psB,
        nc.psum_tensor(f"a0_pt{u}", [128, 1024], BF16) as pt_ps,
        nc.psum_tensor(f"a0_o{u}", [128, 512], F32) as o_ps,
        nc.Block(f"a0_{u}") as b,
    ):
        s_ps = [s_psA, s_psB]

        def q_sl(qc, h):
            base = (h % 2) * 64
            return q_sb[base:base + 64,
                        (qc % 2) * D + (h // 2) * 128:(qc % 2) * D + (h // 2) * 128 + 128]

        def k_sl(h, slot):
            base = (h % 2) * 64
            c = h // 2
            return k_ring[base:base + 64,
                          c * (RING * 128) + slot * 128: c * (RING * 128) + slot * 128 + 128]

        def vt_sl(h, slot):
            base = slot * (NCD * 128) + (h // 2) * 128 + (h % 2) * 64
            return vt_ring[:, base: base + HD]

        @b.sync
        def _(e):
            for qc in range(NQC):
                if qc >= 2:
                    e.wait_ge(mmS, 16 * (qc - 1))
                e.dma_start(
                    q_sb[:, (qc % 2) * D:(qc % 2) * D + D].rearrange(
                        "p (c j) -> p c j", c=NCD),
                    p.qf[:, :, qc * 128:(qc + 1) * 128].rearrange("c p t -> p c t"),
                ).then_inc(qld, 16)
                e.wait_ge(qld, 16 * (qc + 1))
                slot = qc % RING
                if qc >= 2:
                    e.wait_ge(pvd, 16 * (qc - 1))
                e.dma_start(
                    k_ring[:].rearrange("p (c r) -> p c r", c=NCD)[
                        :, :, slot * 128:(slot + 1) * 128],
                    p.kf[:, :, qc * 128:(qc + 1) * 128].rearrange("c p t -> p c t"),
                ).then_inc(kld, 16)
                e.wait_ge(kld, 16 * (qc + 1))
                if qc >= 2:
                    e.wait_ge(vtd, qc - 1)
                e.dma_start(
                    vs[:, (qc % 2) * D:(qc % 2) * D + D].rearrange(
                        "p (c j) -> p c j", c=NCD),
                    p.vf[:, :, qc * 128:(qc + 1) * 128].rearrange("c p t -> p c t"),
                ).then_inc(vld, 16)
                e.wait_ge(vld, 16 * (qc + 1))
                e.wait_ge(oev, 2 * qc + 2)
                e.dma_start(
                    p.of[:, :, qc * 128:(qc + 1) * 128].rearrange("c p t -> p c t"),
                    ofs[:, (qc % 2) * D:(qc % 2) * D + D].rearrange(
                        "p (c j) -> p c j", c=NCD),
                ).then_inc(ofd, 16)
                e.wait_ge(ofd, 16 * (qc + 1))
            e.wait_ge(ofd, 16 * NQC)

        @b.tensor
        def _(e):
            for qc in range(NQC):
                n = nkc(qc)
                w = n * 128
                kstart = max(0, qc - 8)
                gh0 = qc * 16
                e.wait_ge(vld, 16 * (qc + 1))
                e.wait_ge(ptev, ptev_cum_qc[qc])
                for c in range(NCD):
                    tr = e.transpose(
                        pt_ps[:, c * 128:(c + 1) * 128],
                        vs[:, (qc % 2) * D + c * 128:(qc % 2) * D + c * 128 + 128],
                        p.ident_b[:])
                tr.then_inc(vtd, 1)

                e.wait_ge(qld, 16 * (qc + 1))
                e.wait_ge(kld, 16 * (qc + 1))

                def scores(h):
                    gh = gh0 + h
                    if gh >= 2:
                        e.wait_ge(biasd, gh - 1)
                    for j in range(n):
                        kc = kstart + j
                        mm = e.matmul(
                            s_ps[gh % 2][:, j * 128:(j + 1) * 128],
                            q_sl(qc, h), k_sl(h, kc % RING),
                            start=True, stop=True)
                    mm.then_inc(mmS, 1)

                scores(0)
                if H > 1:
                    scores(1)
                for h in range(H):
                    gh = gh0 + h
                    e.wait_ge(normd, gh + 1)
                    nra = min(n, 8)
                    e.wait_ge(ptev, (ptev_cum_head[gh - 1] if gh > gh0
                                     else ptev_cum_qc[qc] + 1))
                    for j in range(nra):
                        tr = e.transpose(
                            pt_ps[:, j * 128:(j + 1) * 128],
                            pn_sb[:, (h % 2) * 1152 + j * 128:(h % 2) * 1152 + (j + 1) * 128],
                            p.ident_b[:])
                    tr.then_inc(pta, 1)
                    if n > 8:
                        e.wait_ge(ptev, ptev_cum_head[gh] - 1)
                        tr = e.transpose(
                            pt_ps[:, 0:128],
                            pn_sb[:, (h % 2) * 1152 + 8 * 128:(h % 2) * 1152 + 9 * 128],
                            p.ident_b[:])
                        tr.then_inc(pta, 1)
                    e.wait_ge(ptev, ptev_cum_head[gh])
                    if h == 0 or h == 8:
                        e.wait_ge(oev, 2 * qc + (1 if h == 8 else 0))
                    obase = (h % 2) * 64
                    ocol = ((h // 2) % 4) * 128
                    for j in range(n):
                        kc = kstart + j
                        mm = e.matmul(
                            o_ps[obase:obase + 64, ocol:ocol + 128],
                            vt_sl(h, kc % RING),
                            pt_sb[:, j * 128:(j + 1) * 128],
                            start=(j == 0), stop=(j == n - 1))
                    mm.then_inc(pvd, 1)
                    if h + 2 < H:
                        scores(h + 2)

        @b.vector
        def _(e):
            for qc in range(NQC):
                n = nkc(qc)
                w = n * 128
                for h in range(H):
                    gh = qc * 16 + h
                    pe_parity = gh % 2
                    e.wait_ge(mmS, gh + 1)
                    if gh >= 2:
                        e.wait_ge(expd, gh - 1)
                    ssl = s_sb[:, pe_parity * 1152: pe_parity * 1152 + w]
                    e.scalar_tensor_tensor(
                        ssl, p.b0[:, 1152 - w:], slopes[h],
                        s_ps[pe_parity][:, 0:w],
                        ALU.mult, ALU.add).then_inc(biasd, 1)
                    e.wait_ge(biasd, gh + 1)
                    e.tensor_reduce(negm[:, pe_parity:pe_parity + 1], ssl,
                                    AX.X, ALU.max, negate=True).then_inc(maxd, 1)
                    e.wait_ge(expd, gh + 1)
                    e.reciprocal(rinv[:, pe_parity:pe_parity + 1],
                                 sums[:, pe_parity:pe_parity + 1]).then_inc(recd, 1)

        @b.scalar
        def _(e):
            def ptevac(qc, h):
                gh = qc * 16 + h
                n = nkc(qc)
                nra = min(n, 8)
                e.wait_ge(pta, pta_cum[gh] + 1)
                e.activation(pt_sb[:, 0:nra * 128], pt_ps[:, 0:nra * 128],
                             AF.Identity).then_inc(ptev, 1)
                if n > 8:
                    e.wait_ge(pta, pta_cum[gh + 1])
                    e.activation(pt_sb[:, 1024:1152], pt_ps[:, 0:128],
                                 AF.Identity).then_inc(ptev, 1)

            def oevac(qc, half):
                e.wait_ge(pvd, qc * 16 + 8 * (half + 1))
                if qc >= 2 and half == 0:
                    e.wait_ge(ofd, 16 * (qc - 1))
                e.activation(
                    ofs[:, (qc % 2) * D + half * 512:(qc % 2) * D + half * 512 + 512],
                    o_ps[:], AF.Identity).then_inc(oev, 1)

            for qc in range(NQC):
                n = nkc(qc)
                w = n * 128
                e.wait_ge(vtd, qc + 1)
                if qc >= 2:
                    e.wait_ge(pvd, 16 * (qc - 1))
                slot = qc % RING
                e.activation(
                    vt_ring[:, slot * (H * HD): (slot + 1) * (H * HD)],
                    pt_ps[:, 0:H * HD], AF.Identity).then_inc(ptev, 1)
                for h in range(H):
                    gh = qc * 16 + h
                    parity = gh % 2
                    e.wait_ge(maxd, gh + 1)
                    if gh >= 2:
                        e.wait_ge(normd, gh - 1)
                    e.activation(
                        p_sb[:, parity * 1152: parity * 1152 + w],
                        s_sb[:, parity * 1152: parity * 1152 + w],
                        AF.Exp, bias=negm[:, parity:parity + 1],
                        accum_out=sums[:, parity:parity + 1]).then_inc(expd, 1)
                    if h >= 1:
                        ptevac(qc, h - 1)
                        if h - 1 == 7:
                            oevac(qc, 0)
                ptevac(qc, H - 1)
                oevac(qc, 1)

        @b.gpsimd
        def _(e):
            for qc in range(NQC):
                n = nkc(qc)
                w = n * 128
                for h in range(H):
                    gh = qc * 16 + h
                    parity = gh % 2
                    e.wait_ge(recd, gh + 1)
                    if gh >= 2:
                        e.wait_ge(pta, pta_cum[gh - 1])
                    e.tensor_scalar(
                        pn_sb[:, parity * 1152: parity * 1152 + w],
                        p_sb[:, parity * 1152: parity * 1152 + w],
                        rinv[:, parity:parity + 1], 1.0,
                        ALU.mult, ALU.mult).then_inc(normd, 1)

    _dump(p, "of0", p.of, [NCD, 128, S], BF16)


def emit_attn1(p):
    nc = p.nc
    u = p.uid()
    qld = p.sem("a1_qld")
    kld = p.sem("a1_kld")
    vld = p.sem("a1_vld")
    ktd = p.sem("a1_ktd")
    vtd = p.sem("a1_vtd")
    tpev = p.sem("a1_tpev")
    atd = p.sem("a1_atd")
    maskd = p.sem("a1_maskd")
    od = p.sem("a1_od")
    dend = p.sem("a1_dend")
    ud = p.sem("a1_ud")
    rdend = p.sem("a1_rdend")
    bcd = p.sem("a1_bcd")
    oevd = p.sem("a1_oevd")
    divd = p.sem("a1_divd")
    kvd = p.sem("a1_kvd")
    dch = p.sem("a1_dch")
    inis = p.sem("a1_inis")
    ofd = p.sem("a1_ofd")

    NC = NQC
    repl2 = set()
    for d in p.dbg:
        if isinstance(d, str) and d.startswith("a1r:"):
            repl2.add(d.split(":")[1])

    with (
        nc.sbuf_tensor(f"a1_q{u}", [128, 2 * D], BF16) as q_cols,
        nc.sbuf_tensor(f"a1_k{u}", [128, 2 * D], BF16) as k_cols,
        nc.sbuf_tensor(f"a1_v{u}", [128, 2 * D], BF16) as v_cols,
        nc.sbuf_tensor(f"a1_kt{u}", [128, 2 * D], BF16) as kt_sb,
        nc.sbuf_tensor(f"a1_vt{u}", [128, 2 * D], BF16) as vt_sb,
        nc.sbuf_tensor(f"a1_at{u}", [128, H * 128], BF16) as at_sb,
        nc.sbuf_tensor(f"a1_osb{u}", [128, NCD * 128], F32) as o_sb,
        nc.sbuf_tensor(f"a1_rd{u}", [128, NCD * 128], F32) as rden_sb,
        nc.sbuf_tensor(f"a1_ofs{u}", [128, 2 * D], BF16) as of_sb,
        nc.sbuf_tensor(f"a1_kv32{u}", [128, 512], F32) as kv32,
        nc.sbuf_tensor(f"a1_kvbf{u}", [128, 512], BF16) as kvbf,
        nc.sbuf_tensor(f"a1_k1c{u}", [128, NCD], F32) as k1c,
        nc.sbuf_tensor(f"a1_k132{u}", [128, NCD], F32) as k132,
        nc.sbuf_tensor(f"a1_k1bf{u}", [128, NCD], BF16) as k1bf,
        nc.psum_tensor(f"a1_tp{u}", [128, 1024], BF16) as tp_ps,
        nc.psum_tensor(f"a1_atp{u}", [128, 1024], F32) as at_ps,
        nc.psum_tensor(f"a1_op{u}", [128, 1024], F32) as o_ps,
        nc.psum_tensor(f"a1_up{u}", [128, 512], F32) as u_ps,
        nc.psum_tensor(f"a1_dp{u}", [128, 1024], F32) as den_ps,
        nc.Block(f"a1_{u}") as b,
    ):
        def q_sl(h, par):
            base = (h % 2) * 64
            return q_cols[base:base + 64,
                          par * D + (h // 2) * 128: par * D + (h // 2) * 128 + 128]

        def k_sl(h, par):
            base = (h % 2) * 64
            return k_cols[base:base + 64,
                          par * D + (h // 2) * 128: par * D + (h // 2) * 128 + 128]

        def kt_sl(h, par):
            base = par * D + (h // 2) * 128 + (h % 2) * 64
            return kt_sb[:, base: base + HD]

        def vt_sl(h, par):
            base = par * D + (h // 2) * 128 + (h % 2) * 64
            return vt_sb[:, base: base + HD]

        def kv_sl(t, h):
            return t[(h % 2) * 64:(h % 2) * 64 + 64,
                     (h // 2) * HD:(h // 2) * HD + HD]

        def o_sl(t, h):
            return t[(h % 2) * 64:(h % 2) * 64 + 64,
                     (h // 2) * 128:(h // 2) * 128 + 128]

        def den_sl(t, h):
            r = (h % 2) * 64
            return t[r:r + 1, (h // 2) * 128:(h // 2) * 128 + 128]

        def at_col(hh):
            return (hh % 2) * 512 + (hh // 2) * 128

        @b.sync
        def _(e):
            for c in range(NC):
                par = c % 2
                if c >= 2:
                    e.wait_ge(dend, 2 * (c - 1))
                e.dma_start(
                    q_cols[:, par * D:par * D + D].rearrange("p (c j) -> p c j", c=NCD),
                    p.qf[:, :, c * 128:(c + 1) * 128].rearrange("c p t -> p c t"),
                ).then_inc(qld, 16)
                e.wait_ge(qld, 16 * (c + 1))
                if c >= 2:
                    e.wait_ge(ktd, c - 1)
                    e.wait_ge(dch, 5 * (c - 2) + 4)
                e.dma_start(
                    k_cols[:, par * D:par * D + D].rearrange("p (c j) -> p c j", c=NCD),
                    p.kf[:, :, c * 128:(c + 1) * 128].rearrange("c p t -> p c t"),
                ).then_inc(kld, 16)
                e.wait_ge(kld, 16 * (c + 1))
                if c >= 2:
                    e.wait_ge(vtd, c - 1)
                e.dma_start(
                    v_cols[:, par * D:par * D + D].rearrange("p (c j) -> p c j", c=NCD),
                    p.vf[:, :, c * 128:(c + 1) * 128].rearrange("c p t -> p c t"),
                ).then_inc(vld, 16)
                e.wait_ge(vld, 16 * (c + 1))
                e.wait_ge(divd, c + 1)
                if "noof" in repl2:
                    e.dma_start(p.of[0, :, 0:64], of_sb[:, 0:64]).then_inc(ofd, 16)
                else:
                    e.dma_start(
                        p.of[:, :, c * 128:(c + 1) * 128].rearrange("c p t -> p c t"),
                        of_sb[:, par * D:par * D + D].rearrange("p (c j) -> p c j", c=NCD),
                    ).then_inc(ofd, 16)
                e.wait_ge(ofd, 16 * (c + 1))

        cut = 99
        repl = set()
        for d in p.dbg:
            if isinstance(d, str) and d.startswith("a1cut"):
                cut = int(d.split(":")[1])
            if isinstance(d, str) and d.startswith("a1r:"):
                repl.add(d.split(":")[1])

        @b.tensor
        def _(e):
            def stub(region2, par, width=512):
                return e.matmul(region2, k_cols[:, par * D:par * D + 128],
                                q_cols[:, par * D:par * D + width],
                                start=True, stop=True)

            for c in range(NC):
                par = c % 2
                e.wait_ge(kld, 16 * (c + 1))
                e.wait_ge(tpev, 2 * c)
                if "notr" in repl:
                    e.matmul(u_ps[:], k_cols[:, par * D:par * D + 128],
                             q_cols[:, par * D:par * D + 512],
                             start=True, stop=True).then_inc(ktd, 1)
                else:
                    for cb in range(NCD):
                        tr = e.transpose(tp_ps[:, cb * 128:(cb + 1) * 128],
                                         k_cols[:, par * D + cb * 128:par * D + cb * 128 + 128],
                                         p.ident_b[:])
                    tr.then_inc(ktd, 1)
                e.wait_ge(vld, 16 * (c + 1))
                e.wait_ge(tpev, 2 * c + 1)
                if "notr" in repl:
                    e.matmul(u_ps[:], k_cols[:, par * D:par * D + 128],
                             q_cols[:, par * D:par * D + 512],
                             start=True, stop=True).then_inc(vtd, 1)
                else:
                    for cb in range(NCD):
                        tr = e.transpose(tp_ps[:, cb * 128:(cb + 1) * 128],
                                         v_cols[:, par * D + cb * 128:par * D + cb * 128 + 128],
                                         p.ident_b[:])
                    tr.then_inc(vtd, 1)
                e.wait_ge(qld, 16 * (c + 1))
                for half in range(2):
                    if c > 0 and half == 0:
                        e.wait_ge(divd, c)
                    if half == 1:
                        e.wait_ge(maskd, 2 * c + 1)
                    if cut >= 2:
                        for hh in range(8):
                            h = half * 8 + hh
                            mm = e.matmul(at_ps[:, at_col(hh):at_col(hh) + 128],
                                          k_sl(h, par), q_sl(h, par),
                                          start=True, stop=True)
                    else:
                        stub(at_ps[:, 0:512], par)
                        mm = stub(at_ps[:, 512:1024], par)
                    mm.then_inc(atd, 1)
                    e.wait_ge(maskd, 2 * c + half + 1)
                    e.wait_ge(tpev, 2 * c + 2)
                    e.wait_ge(kvd, c + 1)
                    if half == 0:
                        e.wait_ge(oevd, c)
                        e.wait_ge(rdend, c)
                    if cut >= 3:
                        for hh in range(8):
                            h = half * 8 + hh
                            acol = half * 1024 + at_col(hh)
                            e.matmul(o_sl(o_ps, h), vt_sl(h, par),
                                     at_sb[:, acol:acol + 128],
                                     start=True, stop=False)
                            e.matmul(o_sl(o_ps, h), kv_sl(kvbf, h), q_sl(h, par),
                                     start=False, stop=True)
                            if cut >= 4:
                                e.matmul(den_sl(den_ps, h), p.ones_b[:],
                                         at_sb[:, acol:acol + 128],
                                         start=True, stop=False)
                                mm = e.matmul(den_sl(den_ps, h),
                                              k1bf[(h % 2) * 64:(h % 2) * 64 + 64,
                                                   h // 2:h // 2 + 1],
                                              q_sl(h, par),
                                              start=False, stop=True)
                            else:
                                mm = stub(den_ps[:, half * 512:half * 512 + 512], par)
                    else:
                        stub(o_ps[:, half * 512:half * 512 + 512], par)
                        mm = stub(den_ps[:, half * 512:half * 512 + 512], par)
                    mm.then_inc(dend, 1)
                if cut >= 5:
                    for h in range(H):
                        mm = e.matmul(kv_sl(u_ps, h), kt_sl(h, par), vt_sl(h, par),
                                      start=True, stop=True)
                else:
                    mm = stub(u_ps[:], par)
                mm.then_inc(ud, 1)
                e.wait_ge(rdend, c + 1)
                e.wait_ge(maskd, 2 * c + 2)
                if cut >= 6:
                    for h in range(0, H, 2):
                        mm = e.matmul(o_sl(at_ps, h),
                                      p.ones4[0:1, 0:64],
                                      rden_sb[0:1, (h // 2) * 128:(h // 2) * 128 + 128],
                                      start=True, stop=True)
                    mm = e.matmul(den_ps[:, 0:64], k_cols[:, par * D:par * D + 128],
                                  q_cols[:, par * D:par * D + 64],
                                  start=True, stop=True)
                    for h in range(1, H, 2):
                        mm = e.matmul(o_sl(at_ps, h),
                                      p.ones4[64:65, 0:64],
                                      rden_sb[64:65, (h // 2) * 128:(h // 2) * 128 + 128],
                                      start=True, stop=True)
                else:
                    stub(at_ps[:, 0:512], par)
                    mm = stub(at_ps[:, 512:1024], par)
                mm.then_inc(bcd, 1)

        @b.scalar
        def _(e):
            for c in range(NC):
                par = c % 2
                e.wait_ge(ktd, c + 1)
                if c >= 2:
                    e.wait_ge(ud, c - 1)
                e.activation(kt_sb[:, par * D:par * D + D], tp_ps[:],
                             AF.Identity).then_inc(tpev, 1)
                e.wait_ge(vtd, c + 1)
                e.activation(vt_sb[:, par * D:par * D + D], tp_ps[:],
                             AF.Identity).then_inc(tpev, 1)
                e.wait_ge(dend, 2 * c + 2)
                e.wait_ge(divd, c)
                if "oev" in repl:
                    e.activation(o_sb[:, 0:512], o_ps[:, 0:512], AF.Identity
                                 ).then_inc(tpev, 0, skip_validation=True) if False else None
                    e.activation(o_sb[:, 0:512], o_ps[:, 0:512], AF.Identity)
                    e.activation(o_sb[:, 512:1024], o_ps[:, 512:1024],
                                 AF.Identity).then_inc(oevd, 1)
                else:
                    e.activation(o_sb[:], o_ps[:], AF.Identity).then_inc(oevd, 1)

        @b.vector
        def _(e):
            nd = 0
            for c in range(NC):
                par = c % 2
                for half in range(2):
                    e.wait_ge(atd, 2 * c + half + 1)
                    e.tensor_mul(at_sb[:, half * 1024:(half + 1) * 1024],
                                 at_ps[:], p.mask16[:, 0:1024]).then_inc(maskd, 1)
                e.wait_ge(dend, 2 * c + 2)
                if c > 0:
                    e.wait_ge(bcd, c)
                if "recip" in repl:
                    e.tensor_copy(rden_sb[0:1, :], den_ps[0:1, :]).then_inc(dch, 1)
                    nd += 1
                    e.tensor_copy(rden_sb[64:65, :], den_ps[64:65, :]).then_inc(rdend, 1)
                else:
                    e.reciprocal(rden_sb[0:1, :], den_ps[0:1, :]).then_inc(dch, 1)
                    nd += 1
                    e.reciprocal(rden_sb[64:65, :], den_ps[64:65, :]).then_inc(rdend, 1)
                e.wait_ge(bcd, c + 1)
                e.wait_ge(oevd, c + 1)
                if c >= 2:
                    e.wait_ge(ofd, 16 * (c - 1))
                e.tensor_mul(of_sb[:, par * D:par * D + D], o_sb[:],
                             at_ps[:]).then_inc(divd, 1)
                e.wait_ge(ud, c + 1)
                if c == 0:
                    e.wait_ge(kvd, 1)
                if "kvadd" in repl:
                    e.tensor_copy(kv32[:], u_ps[:]).then_inc(dch, 1)
                else:
                    e.tensor_add(kv32[:], kv32[:], u_ps[:]).then_inc(dch, 1)
                nd += 1
                e.wait_ge(dch, nd)
                e.tensor_copy(kvbf[:], kv32[:]).then_inc(dch, 1)
                nd += 1
                e.wait_ge(kld, 16 * (c + 1))
                if "red" in repl:
                    e.memset(k1c[:], 1.0).then_inc(dch, 1)
                else:
                    e.tensor_reduce(k1c[:], k_cols[:, par * D:par * D + D].rearrange(
                        "p (c j) -> p c j", c=NCD), AX.X, ALU.add).then_inc(dch, 1)
                nd += 1
                e.wait_ge(dch, nd)
                if "kvadd" in repl:
                    e.tensor_copy(k132[:], k1c[:]).then_inc(dch, 1)
                else:
                    e.tensor_add(k132[:], k132[:], k1c[:]).then_inc(dch, 1)
                nd += 1
                e.wait_ge(dch, nd)
                e.tensor_copy(k1bf[:], k132[:]).then_inc(kvd, 1)

        if "nogp" not in repl2:
            @b.gpsimd
            def _(e):
                e.memset(kv32[:], 0.0).then_inc(inis, 1)
                e.wait_ge(inis, 1)
                e.memset(kvbf[:], 0.0).then_inc(inis, 1)
                e.wait_ge(inis, 2)
                e.memset(k132[:], 0.0).then_inc(inis, 1)
                e.wait_ge(inis, 3)
                e.memset(k1bf[:], 0.0).then_inc(kvd, 1)
        else:
            @b.vector
            def _(e):
                e.memset(kv32[:], 0.0).then_inc(inis, 1)
                e.wait_ge(inis, 1)
                e.memset(kvbf[:], 0.0).then_inc(inis, 1)
                e.wait_ge(inis, 2)
                e.memset(k132[:], 0.0).then_inc(inis, 1)
                e.wait_ge(inis, 3)
                e.memset(k1bf[:], 0.0).then_inc(kvd, 1)

    _dump(p, "of1", p.of, [NCD, 128, S], BF16)


def _dump(p, name, src, shape, dtype):
    if name not in p.dbg:
        return
    dump = p.nc.dram_tensor(f"dbg_{name}", shape, dtype, kind="ExternalOutput")
    sd = p.sem("dump")
    with p.nc.Block() as b:
        @b.sync
        def _(e):
            e.dma_start(dump.ap(), src.ap()).then_inc(sd, 16)
            e.wait_ge(sd, 16)


def emit_ffna(p, l, xn_sb):
    nc = p.nc
    u = p.uid()
    w1ld = p.sem("fa_w1ld")
    w2ld = p.sem("fa_w2ld")
    mm1 = p.sem("fa_mm1")
    mm2 = p.sem("fa_mm2")
    sld = p.sem("fa_sld")
    gd = p.sem("fa_gd")
    std = p.sem("fa_std")
    s1v = p.vec[f"s1{l}"]
    s2v = p.vec[f"s2{l}"]
    w2cd = p.sem("fa_w2cd")
    w2ldp = [p.sem("fa_w2ld0"), p.sem("fa_w2ld1")]
    with (
        nc.sbuf_tensor(f"fa_w1{u}", [128, NCD * DFF], BF16) as w1sb,
        nc.sbuf_tensor(f"fa_w2{u}", [128, 2 * NCD * 128], BF16) as w2sb,
        nc.sbuf_tensor(f"fa_w2f{u}", [128, 2 * NCD * 128], F32) as w2f32,
        nc.sbuf_tensor(f"fa_t1{u}", [128, 2 * TG], F32) as t1,
        nc.sbuf_tensor(f"fa_st{u}", [128, 2 * S], BF16) as stg,
        nc.psum_tensor(f"fa_p1{u}", [128, 2 * TG], F32) as p1,
        nc.psum_tensor(f"fa_p2{u}", [128, 2 * TG], F32) as p2,
        nc.Block(f"fa{u}") as b,
    ):
        @b.gpsimd
        def _(e):
            e.dma_start(w1sb[:], p.wt[f"w1{l}"].ap().rearrange("p c o -> p (c o)"),
                        max_dma_last_dim=8192).then_inc(w1ld, 16)
            e.wait_ge(w1ld, 16)

        @b.tensor
        def _(e):
            e.wait_ge(w1ld, 16)
            un = 0
            for oc in range(NCF):
                e.wait_ge(w2cd, oc + 1)
                for tg in range(NTG):
                    un += 1
                    pa = p1[:, (un % 2) * TG:(un % 2) * TG + TG]
                    pb = p2[:, (un % 2) * TG:(un % 2) * TG + TG]
                    if un > 2:
                        e.wait_ge(sld, un - 2)
                    for ic in range(NCD):
                        mm = e.matmul(
                            pa,
                            w1sb[:, ic * DFF + oc * 128: ic * DFF + oc * 128 + 128],
                            xn_sb[:, ic * S + tg * TG: ic * S + (tg + 1) * TG],
                            start=(ic == 0), stop=(ic == NCD - 1))
                    mm.then_inc(mm1, 1)
                    if un > 2:
                        e.wait_ge(gd, un - 2)
                    for ic in range(NCD):
                        mm = e.matmul(
                            pb,
                            w2sb[:, (oc % 2) * 1024 + ic * 128:(oc % 2) * 1024 + ic * 128 + 128],
                            xn_sb[:, ic * S + tg * TG: ic * S + (tg + 1) * TG],
                            start=(ic == 0), stop=(ic == NCD - 1))
                    mm.then_inc(mm2, 1)

        @b.scalar
        def _(e):
            un = 0
            for oc in range(NCF):
                for tg in range(NTG):
                    un += 1
                    e.wait_ge(mm1, un)
                    if un > 2:
                        e.wait_ge(gd, un - 2)
                    pa = p1[:, (un % 2) * TG:(un % 2) * TG + TG]
                    e.activation(t1[:, (un % 2) * TG:(un % 2) * TG + TG], pa,
                                 AF.Silu, scale=s1v[:, oc:oc + 1]).then_inc(sld, 1)

        @b.vector
        def _(e):
            def w2cast(oc):
                e.wait_ge(w2ldp[oc % 2], 16 * (oc // 2 + 1))
                if oc >= 2:
                    e.wait_ge(mm2, (oc - 1) * NTG)
                e.tensor_copy(w2sb[:, (oc % 2) * 1024:(oc % 2) * 1024 + 1024],
                              w2f32[:, (oc % 2) * 1024:(oc % 2) * 1024 + 1024]
                              ).then_inc(w2cd, 1)

            w2cast(0)
            w2cast(1)
            un = 0
            for oc in range(NCF):
                if oc >= 2:
                    e.wait_ge(std, 16 * (oc - 1))
                for tg in range(NTG):
                    un += 1
                    e.wait_ge(mm2, un)
                    e.wait_ge(sld, un)
                    pb = p2[:, (un % 2) * TG:(un % 2) * TG + TG]
                    out = stg[:, (oc % 2) * S + tg * TG:(oc % 2) * S + (tg + 1) * TG]
                    e.scalar_tensor_tensor(
                        out, pb, s2v[:, oc:oc + 1],
                        t1[:, (un % 2) * TG:(un % 2) * TG + TG],
                        ALU.mult, ALU.mult).then_inc(gd, 1)
                if oc + 2 < NCF:
                    w2cast(oc + 2)

        @b.sync
        def _(e):
            def w2load(oc):
                if oc >= 2:
                    e.wait_ge(w2cd, oc - 1)
                e.dma_start(w2f32[:, (oc % 2) * 1024:(oc % 2) * 1024 + 1024],
                            p.wt[f"w2{l}"][oc, :, :]).then_inc(w2ldp[oc % 2], 16)

            w2load(0)
            w2load(1)
            for oc in range(NCF):
                if oc + 2 < NCF:
                    w2load(oc + 2)
                e.wait_ge(gd, (oc + 1) * NTG)
                e.dma_start(p.g[oc, :, :],
                            stg[:, (oc % 2) * S:(oc % 2) * S + S]).then_inc(std, 16)
                e.wait_ge(std, 16 * (oc + 1))

    _dump(p, f"g{l}", p.g, [NCF, 128, S], BF16)


def emit_ffnb(p, l):
    nc = p.nc
    u = p.uid()
    gld = p.sem("fb_gld")
    wld = p.sem("fb_wld")
    mmd = p.sem("fb_mmd")
    evd = p.sem("fb_evd")
    std = p.sem("fb_std")
    hs1 = p.vec[f"hs1{l}"]
    hs2 = p.vec[f"hs2{l}"]
    NGT = FT // TG
    wcd = p.sem("fb_wcd")
    wldp = [p.sem("fb_wld0"), p.sem("fb_wld1")]
    NU = NFT * 64

    def w_src(p_, un):
        ft, r = divmod(un, 64)
        st, oc = divmod(r, NCF)
        return (p_.wt[f"h1{l}"] if st == 0 else p_.wt[f"h2{l}"])[oc, :, :]

    with (
        nc.sbuf_tensor(f"fb_g{u}", [128, NCF * FT], BF16) as gt,
        nc.sbuf_tensor(f"fb_h{u}", [128, NCF * FT], BF16) as h1t,
        nc.sbuf_tensor(f"fb_w{u}", [128, 2 * DFF], BF16) as wsb,
        nc.sbuf_tensor(f"fb_wf{u}", [128, 2 * DFF], F32) as wf32,
        nc.sbuf_tensor(f"fb_st{u}", [128, 2 * FT], BF16) as stg,
        nc.psum_tensor(f"fb_ps{u}", [128, 2 * TG], F32) as ps,
        nc.Block(f"fb{u}") as b,
    ):
        @b.sync
        def _(e):
            def wload(un):
                if un >= 2:
                    e.wait_ge(wcd, un - 1)
                e.dma_start(wf32[:, (un % 2) * DFF:(un % 2) * DFF + DFF],
                            w_src(p, un)).then_inc(wldp[un % 2], 16)

            for ft in range(NFT):
                if ft > 0:
                    e.wait_ge(mmd, (ft * 64 - 32) * NGT)
                e.dma_start(
                    gt[:].rearrange("p (c t) -> p c t", c=NCF),
                    p.g[:, :, ft * FT:(ft + 1) * FT].rearrange("c p t -> p c t"),
                ).then_inc(gld, 16)
                if ft == 0:
                    wload(0)
                    wload(1)
                for st in range(2):
                    for oc in range(NCF):
                        un = ft * 64 + st * 32 + oc
                        if un + 2 < NU:
                            wload(un + 2)
                        if st == 1:
                            e.wait_ge(evd, un * NGT + NGT)
                            e.dma_start(p.h2a[oc, :, ft * FT:(ft + 1) * FT],
                                        stg[:, (oc % 2) * FT:(oc % 2) * FT + FT]
                                        ).then_inc(std, 16)
                            e.wait_ge(std, 16 * (ft * NCF + oc + 1))

        @b.vector
        def _(e):
            for un in range(NU):
                e.wait_ge(wldp[un % 2], 16 * (un // 2 + 1))
                if un >= 2:
                    e.wait_ge(mmd, (un - 1) * NGT)
                e.tensor_copy(wsb[:, (un % 2) * DFF:(un % 2) * DFF + DFF],
                              wf32[:, (un % 2) * DFF:(un % 2) * DFF + DFF]
                              ).then_inc(wcd, 1)

        @b.tensor
        def _(e):
            for ft in range(NFT):
                e.wait_ge(gld, 16 * (ft + 1))
                for st in range(2):
                    if st == 1:
                        e.wait_ge(evd, (ft * 64 + 32) * NGT)
                    for oc in range(NCF):
                        un = ft * 64 + st * 32 + oc
                        e.wait_ge(wcd, un + 1)
                        src = gt if st == 0 else h1t
                        for g in range(NGT):
                            gu = un * NGT + g
                            if gu >= 2:
                                e.wait_ge(evd, gu - 1)
                            pv = ps[:, (gu % 2) * TG:(gu % 2) * TG + TG]
                            for ic in range(NCF):
                                mm = e.matmul(
                                    pv,
                                    wsb[:, (un % 2) * DFF + ic * 128:(un % 2) * DFF + ic * 128 + 128],
                                    src[:, ic * FT + g * TG: ic * FT + g * TG + TG],
                                    start=(ic == 0), stop=(ic == NCF - 1))
                            mm.then_inc(mmd, 1)

        @b.scalar
        def _(e):
            for ft in range(NFT):
                for st in range(2):
                    for oc in range(NCF):
                        un = ft * 64 + st * 32 + oc
                        sv = hs1 if st == 0 else hs2
                        for g in range(NGT):
                            gu = un * NGT + g
                            e.wait_ge(mmd, gu + 1)
                            if st == 0:
                                out = h1t[:, oc * FT + g * TG: oc * FT + g * TG + TG]
                            else:
                                if g == 0 and oc >= 2:
                                    e.wait_ge(std, 16 * (ft * NCF + oc - 1))
                                out = stg[:, (oc % 2) * FT + g * TG:(oc % 2) * FT + g * TG + TG]
                            pv = ps[:, (gu % 2) * TG:(gu % 2) * TG + TG]
                            e.activation(out, pv, AF.Gelu,
                                         scale=sv[:, oc:oc + 1]).then_inc(evd, 1)

    _dump(p, f"h2a{l}", p.h2a, [NCF, 128, S], BF16)


def emit_c(p, l, last):
    nc = p.nc
    u = p.uid()
    wold = p.sem("c_wold")
    w3ld = p.sem("c_w3ld")
    hld = p.sem("c_hld")
    old_ = p.sem("c_old")
    xld = p.sem("c_xld")
    mm3 = p.sem("c_mm3")
    mmo = p.sem("c_mmo")
    tw = p.sem("c_tw")
    t2d = p.sem("c_t2d")
    pld = p.sem("c_pld")
    w3cd = p.sem("c_w3cd")
    w3ldp = [p.sem("c_w3ld0"), p.sem("c_w3ld1")]
    std = p.sem("c_std")
    xsrc = p.x if l == 0 else p.x1
    xdst = p.x1 if l == 0 else p.y
    s3v = p.vec[f"s3{l}"]
    bov = p.vec[f"bo{l}"]
    NGT = FT // TG
    with (
        nc.sbuf_tensor(f"c_h2{u}", [128, NCF * FT], BF16) as h2t,
        nc.sbuf_tensor(f"c_of{u}", [128, NCD * FT], BF16) as oft,
        nc.sbuf_tensor(f"c_x{u}", [128, NCD * FT], F32) as xt,
        nc.sbuf_tensor(f"c_wo{u}", [128, NCD * D], BF16) as wosb,
        nc.sbuf_tensor(f"c_w3{u}", [128, 2 * DFF], BF16) as w3sb,
        nc.sbuf_tensor(f"c_w3f{u}", [128, 2 * DFF], F32) as w3f32,
        nc.sbuf_tensor(f"c_two{u}", [128, 2 * TG], F32) as two,
        nc.sbuf_tensor(f"c_t2{u}", [128, 2 * TG], F32) as t2,
        nc.sbuf_tensor(f"c_st{u}", [128, 2 * FT], F32) as stg,
        nc.psum_tensor(f"c_p3{u}", [128, 2 * TG], F32) as p3,
        nc.psum_tensor(f"c_po{u}", [128, 2 * TG], F32) as po,
        nc.Block(f"c{u}") as b,
    ):
        @b.gpsimd
        def _(e):
            e.dma_start(wosb[:], p.wt[f"wo{l}"].ap().rearrange("p c o -> p (c o)"),
                        max_dma_last_dim=8192).then_inc(wold, 16)
            e.wait_ge(wold, 16)
            for ft in range(NFT):
                e.wait_ge(xld, 16 * (ft + 1))
                for oc in range(NCD):
                    un = ft * NCD + oc
                    for g in range(NGT):
                        gu = un * NGT + g
                        e.wait_ge(tw, gu + 1)
                        if gu >= 2:
                            e.wait_ge(t2d, gu - 1)
                        e.tensor_add(
                            t2[:, (gu % 2) * TG:(gu % 2) * TG + TG],
                            two[:, (gu % 2) * TG:(gu % 2) * TG + TG],
                            xt[:, oc * FT + g * TG: oc * FT + g * TG + TG]
                        ).then_inc(pld, 1)

        @b.sync
        def _(e):
            def w3load(un):
                if un >= 2:
                    e.wait_ge(w3cd, un - 1)
                e.dma_start(w3f32[:, (un % 2) * DFF:(un % 2) * DFF + DFF],
                            p.wt[f"w3{l}"][un % NCD, :, :]).then_inc(w3ldp[un % 2], 16)

            for ft in range(NFT):
                if ft > 0:
                    e.wait_ge(mm3, ft * NCD * NGT)
                    e.wait_ge(mmo, ft * NCD * NGT)
                    e.wait_ge(pld, ft * NCD * NGT)
                sl = slice(ft * FT, (ft + 1) * FT)
                e.dma_start(h2t[:].rearrange("p (c t) -> p c t", c=NCF),
                            p.h2a[:, :, sl].rearrange("c p t -> p c t")
                            ).then_inc(hld, 16)
                e.dma_start(oft[:].rearrange("p (c t) -> p c t", c=NCD),
                            p.of[:, :, sl].rearrange("c p t -> p c t")
                            ).then_inc(old_, 16)
                e.dma_start(xt[:].rearrange("p (c t) -> p c t", c=NCD),
                            xsrc[:, :, sl].rearrange("c p t -> p c t")
                            ).then_inc(xld, 16)
                if ft == 0:
                    w3load(0)
                    w3load(1)
                for oc in range(NCD):
                    un = ft * NCD + oc
                    if un + 2 < NFT * NCD:
                        w3load(un + 2)
                    e.wait_ge(t2d, un * NGT + NGT)
                    e.dma_start(xdst[oc, :, sl],
                                stg[:, (oc % 2) * FT:(oc % 2) * FT + FT]
                                ).then_inc(std, 16)
                    e.wait_ge(std, 16 * (ft * NCD + oc + 1))

        @b.tensor
        def _(e):
            e.wait_ge(wold, 16)
            for ft in range(NFT):
                e.wait_ge(hld, 16 * (ft + 1))
                e.wait_ge(old_, 16 * (ft + 1))
                for oc in range(NCD):
                    un = ft * NCD + oc
                    e.wait_ge(w3cd, un + 1)
                    for g in range(NGT):
                        gu = un * NGT + g
                        if gu >= 2:
                            e.wait_ge(t2d, gu - 1)
                            e.wait_ge(tw, gu - 1)
                        pv3 = p3[:, (gu % 2) * TG:(gu % 2) * TG + TG]
                        pvo = po[:, (gu % 2) * TG:(gu % 2) * TG + TG]
                        for ic in range(NCF):
                            mm = e.matmul(
                                pv3,
                                w3sb[:, (un % 2) * DFF + ic * 128:(un % 2) * DFF + ic * 128 + 128],
                                h2t[:, ic * FT + g * TG: ic * FT + g * TG + TG],
                                start=(ic == 0), stop=(ic == NCF - 1))
                        mm.then_inc(mm3, 1)
                        for ic in range(NCD):
                            mm = e.matmul(
                                pvo,
                                wosb[:, ic * D + oc * 128: ic * D + oc * 128 + 128],
                                oft[:, ic * FT + g * TG: ic * FT + g * TG + TG],
                                start=(ic == 0), stop=(ic == NCD - 1))
                        mm.then_inc(mmo, 1)

        @b.scalar
        def _(e):
            for ft in range(NFT):
                for oc in range(NCD):
                    for g in range(NGT):
                        gu = (ft * NCD + oc) * NGT + g
                        e.wait_ge(mmo, gu + 1)
                        if gu >= 2:
                            e.wait_ge(pld, gu - 1)
                        pvo = po[:, (gu % 2) * TG:(gu % 2) * TG + TG]
                        e.activation(two[:, (gu % 2) * TG:(gu % 2) * TG + TG],
                                     pvo, AF.Identity,
                                     bias=bov[:, oc:oc + 1]).then_inc(tw, 1)

        @b.vector
        def _(e):
            def w3cast(un):
                e.wait_ge(w3ldp[un % 2], 16 * (un // 2 + 1))
                if un >= 2:
                    e.wait_ge(mm3, (un - 1) * NGT)
                e.tensor_copy(w3sb[:, (un % 2) * DFF:(un % 2) * DFF + DFF],
                              w3f32[:, (un % 2) * DFF:(un % 2) * DFF + DFF]
                              ).then_inc(w3cd, 1)

            w3cast(0)
            w3cast(1)
            for ft in range(NFT):
                for oc in range(NCD):
                    un = ft * NCD + oc
                    if un + 2 < NFT * NCD:
                        w3cast(un + 2)
                    if oc >= 2:
                        e.wait_ge(std, 16 * (ft * NCD + oc - 1))
                    for g in range(NGT):
                        gu = un * NGT + g
                        e.wait_ge(mm3, gu + 1)
                        e.wait_ge(pld, gu + 1)
                        pv3 = p3[:, (gu % 2) * TG:(gu % 2) * TG + TG]
                        e.scalar_tensor_tensor(
                            stg[:, (oc % 2) * FT + g * TG:(oc % 2) * FT + g * TG + TG],
                            pv3, s3v[:, oc:oc + 1],
                            t2[:, (gu % 2) * TG:(gu % 2) * TG + TG],
                            ALU.mult, ALU.add).then_inc(t2d, 1)

    if last and f"x1_{l}" in p.dbg:
        pass
    if not last:
        _dump(p, f"x1_{l}", p.x1, [NCD, 128, S], F32)


def emit_zero_of(p):
    nc = p.nc
    u = p.uid()
    sd = p.sem("zof")
    sz = p.sem("zofc")
    with (
        nc.sbuf_tensor(f"zof{u}", [128, S], BF16) as z,
        nc.Block(f"zof{u}") as b,
    ):
        @b.gpsimd
        def _(e):
            e.memset(z[:], 0.0).then_inc(sz, 1)

        @b.sync
        def _(e):
            e.wait_ge(sz, 1)
            for oc in range(NCD):
                e.dma_start(p.of[oc, :, :], z[:]).then_inc(sd, 16)
                e.wait_ge(sd, 16 * (oc + 1))


def build_program(dbg=None, upto=99):
    nc = bass.Bass("TRN2", target_bir_lowering=False)
    p = P(nc, dbg)
    declare_io(p)
    emit_const(p)
    ph = upto if isinstance(upto, set) else None
    if ph is None:
        ph = set()
        if upto >= 1:
            ph |= {"ln"}
        if upto >= 2:
            ph |= {"qkv"}
        if upto >= 3:
            ph |= {"ffna"}
        if upto >= 4:
            ph |= {"ffnb"}
        if upto >= 5:
            ph |= {"c"}
        if upto >= 6:
            ph |= {"l1"}
        if upto >= 7:
            ph |= {"attn"}
    if "zero_of" in p.dbg:
        emit_zero_of(p)
    p.recycle()
    for l in range(2):
        with nc.sbuf_tensor(f"xn{l}", [128, NCD * S], BF16) as xn_sb:
            emit_ln(p, l, xn_sb)
            p.recycle()
            if "qkv" in ph:
                emit_qkv(p, l, xn_sb)
            if "ffna" in ph:
                emit_ffna(p, l, xn_sb)
                p.recycle()
        if "attn" in ph:
            if l == 0:
                emit_attn0(p)
            else:
                emit_attn1(p)
            p.recycle()
        if "ffnb" in ph:
            emit_ffnb(p, l)
            p.recycle()
        if "c" in ph:
            emit_c(p, l, last=(l == 1))
            p.recycle()
        if "l1" not in ph:
            break
    return nc, p


def kernel(**inputs):
    w = _prep_weights(inputs)
    x = np.asarray(inputs["L0_attn_x"], dtype=np.float32)
    in_maps = []
    for b in range(8):
        m = dict(w)
        m["x"] = np.ascontiguousarray(x[b].T).reshape(NCD, 128, S)
        in_maps.append(m)
    nc, p = build_program()
    res = run_bass_kernel_spmd(nc, in_maps, core_ids=list(range(8)))
    out = np.stack([np.ascontiguousarray(
        np.asarray(res.results[b]["y"]).reshape(D, S).T) for b in range(8)])
    return np.ascontiguousarray(out, dtype=np.float32)


if __name__ == "__main__":
    nc, p = build_program(dbg={"xn0"}, upto=1)
    print("built ok")
